# version 01
# baseline (speedup 1.0000x reference)
import os
import numpy as np
import concourse.bass as bass
import concourse.mybir as mybir
from concourse.bass_utils import run_bass_kernel_spmd
from concourse.bass_core_rust import IndirectOffsetOnAxis
from contextlib import ExitStack

F32 = mybir.dt.float32
BF16 = mybir.dt.bfloat16
U32 = mybir.dt.uint32
AF = mybir.ActivationFunctionType
ALU = mybir.AluOpType
AX = mybir.AxisListType

STAGE = 3

T = 4096
H = 1024
NCH = 8
NT = 32
NE = 32
EPC = 4
CAP = 1280
PCAP = 768
AR = 8 * PCAP
ZROW = float(AR)
NG = CAP // 128
LN_EPS = 1e-5
ATTN_EPS = 1e-6
RG = [[0, 1, 2, 3, 4, 5, 6, 7]]
BLK = [(0, 512), (512, 512), (1024, 256)]


class _CCTok:
    def __init__(self, sem):
        self.sem = sem


class Chain:
    def __init__(self, k, ename, self_sync=True):
        self.k = k
        self.ename = ename
        self.sem = k.ctx.enter_context(k.nc.semaphore(f"ch_{ename}"))
        self.cum = 0
        self.self_sync = self_sync
        self.hw = {}

    def tok(self):
        return (self, self.cum)

    def _emit(self, f):
        import traceback
        site = traceback.extract_stack()[-4:-2]
        def wrapped(e, f=f, site=site):
            try:
                return f(e)
            except Exception as ex:
                raise RuntimeError(f"emit failed; scheduled at {site}") from ex
        self.k.prog[self.ename].append(wrapped)

    def wait(self, tok):
        ch, val = tok
        if val <= 0:
            return
        if ch is self:
            if self.self_sync:
                return
            if self.hw.get(self.sem.num, -1) >= val:
                return
            sem = self.sem
            self._emit(lambda e: e.wait_ge(sem, val))
            self.hw[self.sem.num] = val
            return
        if self.hw.get(ch.sem.num, -1) >= val:
            return
        sem = ch.sem
        self._emit(lambda e: e.wait_ge(sem, val))
        self.hw[ch.sem.num] = val

    def _self_wait(self):
        if self.self_sync and self.cum > 0:
            if self.hw.get(self.sem.num, -1) < self.cum:
                sem, val = self.sem, self.cum
                self._emit(lambda e: e.wait_ge(sem, val))
                self.hw[self.sem.num] = val

    def op(self, f, deps=(), inc=True, step=1):
        for d in deps:
            self.wait(d)
        self._self_wait()
        if inc:
            sem, s = self.sem, step
            self._emit(lambda e: f(e).then_inc(sem, s))
            self.cum += s
        else:
            self._emit(lambda e: f(e))
        return self.tok()

    def dma(self, out, in_, deps=(), **kw):
        try:
            od = out.tensor.space if hasattr(out, "tensor") else None
            idd = in_.tensor.space if hasattr(in_, "tensor") else None
        except Exception:
            od = idd = None
        return self.op(lambda e: e.dma_start(out=out, in_=in_, **kw),
                       deps=deps, inc=True, step=16)

    def dmat(self, out, in_, deps=()):
        return self.op(lambda e: e.dma_start_transpose(out, in_),
                       deps=deps, inc=True, step=16)

    def igather(self, out, in_, idx, deps=(), **kw):
        return self.op(lambda e: e.indirect_dma_start(
            out=out, out_offset=None, in_=in_,
            in_offset=IndirectOffsetOnAxis(idx, 0), **kw),
            deps=deps, inc=True, step=16)

    def iscatter(self, out, idx, in_, deps=(), **kw):
        return self.op(lambda e: e.indirect_dma_start(
            out=out, out_offset=IndirectOffsetOnAxis(idx, 0),
            in_=in_, in_offset=None, **kw),
            deps=deps, inc=True, step=16)


class K:
    def __init__(self, nc, ctx):
        self.nc = nc
        self.ctx = ctx
        self.prog = {n: [] for n in ("sync", "vector", "scalar", "tensor", "gpsimd")}
        self.ve = Chain(self, "vector")
        self.sc = Chain(self, "scalar")
        self.pe = Chain(self, "tensor", self_sync=False)
        self.gp = Chain(self, "gpsimd")
        self.sq = Chain(self, "sync", self_sync=False)
        self.chains = [self.ve, self.sc, self.pe, self.gp, self.sq]
        self._ncc = 0

    def collective(self, kind, ins, outs, deps=()):
        sem = self.ctx.enter_context(self.nc.semaphore(f"cc{self._ncc}"))
        self._ncc += 1
        ch = self.gp
        for d in deps:
            ch.wait(d)
        ch._self_wait()
        ch._emit(lambda e: e.collective_compute(
            kind, ALU.bypass, replica_groups=RG, ins=ins, outs=outs
        ).then_inc(sem, 1))
        return (_CCTok(sem), 1)

    def barrier(self):
        for a in self.chains:
            for b in self.chains:
                a.wait(b.tok())

    def emit(self, block):
        block.sync(lambda e: [f(e) for f in self.prog["sync"]])
        block.vector(lambda e: [f(e) for f in self.prog["vector"]])
        block.scalar(lambda e: [f(e) for f in self.prog["scalar"]])
        block.tensor(lambda e: [f(e) for f in self.prog["tensor"]])
        block.gpsimd(lambda e: [f(e) for f in self.prog["gpsimd"]])


def _consts(nc):
    c = {}
    c["ident"] = nc.inline_tensor(np.eye(128, dtype=np.float32), "c_ident").ap()
    c["iotaB"] = nc.inline_tensor(
        np.tile(np.arange(NE, dtype=np.float32) + 1e6, (128, 1)), "c_iotaB").ap()
    c["iota32col"] = nc.inline_tensor(
        np.arange(NE, dtype=np.float32).reshape(NE, 1), "c_iota32col").ap()
    c["ones_1x128"] = nc.inline_tensor(np.ones((1, 128), np.float32), "c_o1x128").ap()
    c["ones_1x32"] = nc.inline_tensor(np.ones((1, 32), np.float32), "c_o1x32").ap()
    c["ones_32x1"] = nc.inline_tensor(np.ones((32, 1), np.float32), "c_o32x1").ap()
    c["ones_8x1"] = nc.inline_tensor(np.ones((8, 1), np.float32), "c_o8x1").ap()
    c["ones_128x1"] = nc.inline_tensor(np.ones((128, 1), np.float32), "c_o128x1").ap()
    c["iotaJ"] = nc.inline_tensor(
        np.tile(np.arange(CAP, dtype=np.float32), (NE, 1)), "c_iotaJ").ap()
    bsel = np.zeros((NE, EPC), np.float32)
    for p in range(NE):
        bsel[p, p // 8] = 1.0
    c["blocksel"] = nc.inline_tensor(bsel, "c_blocksel").ap()
    c["sbase48"] = nc.inline_tensor(
        np.tile(np.arange(8, dtype=np.float32) * PCAP, (4, 1)), "c_sbase48").ap()
    c["dstb84m1"] = nc.inline_tensor(
        np.tile((np.arange(8, dtype=np.float32) * PCAP).reshape(8, 1), (1, 4)) - 1.0,
        "c_dstb84m1").ap()
    return c


def build(stage=STAGE):
    os.environ["TRN_ALLOW_ICI_ALL_TO_ALL"] = "1"
    nc = bass.Bass("TRN2", target_bir_lowering=False)

    x_in = nc.dram_tensor("x", [T, H], F32, kind="ExternalInput").ap()
    wr_in = nc.dram_tensor("w_router", [H, NE], F32, kind="ExternalInput").ap()
    hf_in = nc.dram_tensor("h_filter", [4, H], F32, kind="ExternalInput").ap()
    gf_in = nc.dram_tensor("g_filter", [4, H], F32, kind="ExternalInput").ap()
    hs_in = nc.dram_tensor("h_synth", [4, H], F32, kind="ExternalInput").ap()
    gs_in = nc.dram_tensor("g_synth", [4, H], F32, kind="ExternalInput").ap()
    gam_in = nc.dram_tensor("gamma", [1, H], F32, kind="ExternalInput").ap()
    bet_in = nc.dram_tensor("beta", [1, H], F32, kind="ExternalInput").ap()
    wg_in = nc.dram_tensor("wg", [EPC, H, H], F32, kind="ExternalInput").ap()
    wu_in = nc.dram_tensor("wu", [EPC, H, H], F32, kind="ExternalInput").ap()
    wd_in = nc.dram_tensor("wd", [EPC, H, H], F32, kind="ExternalInput").ap()
    mprev_in = nc.dram_tensor("mask_prev", [8, 1], F32, kind="ExternalInput").ap()
    osel_in = nc.dram_tensor("own_sel", [NE, EPC], F32, kind="ExternalInput").ap()
    out_dr = nc.dram_tensor("out", [T, H], F32, kind="ExternalOutput").ap()
    DBG = False
    if DBG:
        dbg_strip = nc.dram_tensor("dbg_strip", [NE, 1024 + 32], F32,
                                   kind="ExternalOutput").ap()
        dbg_cols = nc.dram_tensor("dbg_cols", [128, 5 * NT], F32,
                                  kind="ExternalOutput").ap()
        dbg_idx = nc.dram_tensor("dbg_idx", [128, NG * EPC], U32,
                                 kind="ExternalOutput").ap()
        dbg_cnt = nc.dram_tensor("dbg_cnt", [8, NE], F32,
                                 kind="ExternalOutput").ap()
        dbg_sm = nc.dram_tensor("dbg_sm", [NE, 24], F32,
                                kind="ExternalOutput").ap()

    xln_dr = nc.dram_tensor("xln_dr", [T, H], BF16, kind="Internal").ap()
    send_dr = nc.dram_tensor("send_dr", [AR, H], BF16, kind="Internal").ap()
    recv_dr = nc.dram_tensor("recv_dr", [AR + 1, H], BF16, kind="Internal").ap()
    rsrc_dr = nc.dram_tensor("rsrc_dr", [AR + 1, H], BF16, kind="Internal").ap()
    rbuf_dr = nc.dram_tensor("rbuf_dr", [AR + 1, H], BF16, kind="Internal").ap()
    gath_dr = nc.dram_tensor("gath_dr", [2, CAP, H], BF16, kind="Internal").ap()
    r2_dr = nc.dram_tensor("r2_dr", [T, H], F32, kind="Internal").ap()
    r3_dr = nc.dram_tensor("r3_dr", [T, H], F32, kind="Internal").ap()
    s_dr = nc.dram_tensor("s_dr", [H, T], F32, kind="Internal").ap()
    cnt_src = nc.dram_tensor("cnt_src", [1, NE], F32, kind="Internal").ap()
    cnt_dst = nc.dram_tensor("cnt_dst", [8, NE], F32, kind="Internal").ap()
    b32a = nc.dram_tensor("b32a", [NE], F32, kind="Internal").ap()
    b32b = nc.dram_tensor("b32b", [NE], F32, kind="Internal").ap()
    b32c = nc.dram_tensor("b32c", [NE], F32, kind="Internal").ap()

    C = _consts(nc)

    with ExitStack() as top:
        k = K(nc, top)
        ve, sc, pe, gp, sq = k.ve, k.sc, k.pe, k.gp, k.sq

        def sb(name, shape, dt=F32, ctx=top):
            return ctx.enter_context(nc.sbuf_tensor(name, shape, dt))

        ps_a = top.enter_context(nc.psum_tensor("ps_a", [128, 512], F32))
        ps_b = top.enter_context(nc.psum_tensor("ps_b", [128, 512], F32))
        ps_c = top.enter_context(nc.psum_tensor("ps_c", [128, 512], F32))
        ps_d = top.enter_context(nc.psum_tensor("ps_d", [128, 512], F32))
        ps_y = top.enter_context(nc.psum_tensor("ps_y", [128, 1024], F32))
        ps_w = top.enter_context(nc.psum_tensor("ps_w", [128, 1024], F32))

        identt = sb("identt", [128, 128])
        wr_sb = sb("wr_sb", [128, NCH * NE])
        csum_b = sb("csum_b", [128, NE])
        fil = sb("fil", [128, NCH * 4 * 4])
        fsyn = sb("fsyn", [128, NCH * 3 * 2])
        onesr = sb("onesr", [1, 128])
        ones32r = sb("ones32r", [1, 32])
        ones32c = sb("ones32c", [NE, 1])
        ones8c = sb("ones8c", [8, 1])
        ones128c = sb("ones128c", [128, 1])
        iotaBt = sb("iotaBt", [128, NE])
        iota32c = sb("iota32c", [NE, 1])
        mprev = sb("mprev", [8, 1])
        osel = sb("osel", [NE, EPC])
        iotaJt = sb("iotaJt", [NE, CAP])
        bselt = sb("bselt", [NE, EPC])
        sbase48 = sb("sbase48", [4, 8])
        dstb84m1 = sb("dstb84m1", [8, 4])
        mu2_c = sb("mu2_c", [128, NT])
        rho2_c = sb("rho2_c", [128, NT])
        st1 = sb("st1", [128, 1])
        st2 = sb("st2", [128, 1])
        st3 = sb("st3", [128, 1])
        st4 = sb("st4", [128, 1])
        csrow = sb("csrow", [1, NE])

        bc_reg = nc.gpsimd.alloc_register("bc_reg")
        gp.op(lambda e: e.reg_mov(bc_reg, AR - 1), inc=False)
        k.bc_reg = bc_reg

        tk_id = sq.dma(identt[:], C["ident"][:, :])
        sq.dma(onesr[:], C["ones_1x128"][:, :])
        sq.dma(ones32r[:], C["ones_1x32"][:, :])
        sq.dma(ones32c[:], C["ones_32x1"][:, :])
        sq.dma(ones8c[:], C["ones_8x1"][:, :])
        sq.dma(ones128c[:], C["ones_128x1"][:, :])
        sq.dma(iotaBt[:], C["iotaB"][:, :])
        sq.dma(iota32c[:], C["iota32col"][:, :])
        sq.dma(mprev[:], mprev_in[:, :])
        sq.dma(osel[:], osel_in[:, :])
        sq.dma(iotaJt[:], C["iotaJ"][:, :])
        sq.dma(bselt[:], C["blocksel"][:, :])
        sq.dma(sbase48[:], C["sbase48"][:, :])
        sq.dma(dstb84m1[:], C["dstb84m1"][:, :])
        for c in range(NCH):
            sq.dma(wr_sb[:, c * NE:(c + 1) * NE], wr_in[c * 128:(c + 1) * 128, :])
        tk_t0 = sq.tok()

        t0ctx = ExitStack()
        zb = sb("zb", [128, H], BF16, ctx=t0ctx)
        tk_zb = ve.op(lambda e: e.memset(zb[:], 0.0))
        for buf, tot in ((send_dr, AR), (recv_dr, AR + 1), (rsrc_dr, AR + 1),
                         (rbuf_dr, AR + 1)):
            r0 = 0
            while r0 < tot:
                n = min(128, tot - r0)
                sq.dma(buf[r0:r0 + n, :], zb[0:n, :], deps=[tk_zb])
                r0 += n
        tk_zbufs = sq.tok()

        filt_rows = sb("filt_rows", [4, H], ctx=t0ctx)
        for fi, src in enumerate((hf_in, gf_in, hs_in, gs_in)):
            tkl = sq.dma(filt_rows[:], src[:, :], deps=[ve.tok()])
            for c in range(NCH):
                pt = pe.op(lambda e, c=c: e.transpose(
                    ps_a[:, 0:4], filt_rows[0:4, c * 128:(c + 1) * 128],
                    identt[0:4, 0:4]), deps=[tkl, tk_id, ve.tok()], inc=True)
                ve.op(lambda e, fi=fi, c=c: e.tensor_copy(
                    fil[:, (fi * NCH + c) * 4:(fi * NCH + c) * 4 + 4], ps_a[:, 0:4]),
                    deps=[pt])
        for j, fi in enumerate((2, 3)):
            for c in range(NCH):
                b = (fi * NCH + c) * 4
                o = (j * NCH + c) * 3
                ve.op(lambda e, b=b, o=o: e.tensor_add(
                    fsyn[:, o:o + 1], fil[:, b + 1:b + 2], fil[:, b + 2:b + 3]))
                ve.op(lambda e, b=b, o=o: e.tensor_add(
                    fsyn[:, o + 1:o + 2], fil[:, b:b + 1], fil[:, b + 1:b + 2]))
                ve.op(lambda e, b=b, o=o: e.tensor_add(
                    fsyn[:, o + 2:o + 3], fil[:, b + 2:b + 3], fil[:, b + 3:b + 4]))
        tk_fil = ve.tok()

        for c in range(NCH):
            pe.op(lambda e, c=c: e.matmul(
                ps_b[0:1, 0:NE], ones128c[:, 0:1], wr_sb[:, c * NE:(c + 1) * NE],
                start=(c == 0), stop=(c == NCH - 1)),
                deps=([tk_t0] if c == 0 else ()), inc=(c == NCH - 1))
        t = ve.op(lambda e: e.tensor_copy(csrow[:], ps_b[0:1, 0:NE]), deps=[pe.tok()])
        pt = pe.op(lambda e: e.matmul(ps_a[:, 0:NE], onesr[0:1, :], csrow[0:1, :],
                                      start=True, stop=True), deps=[t], inc=True)
        ve.op(lambda e: e.tensor_copy(csum_b[:], ps_a[:, 0:NE]), deps=[pt])
        tk_setup = ve.tok()
        k.barrier()
        t0ctx.close()

        def emit_ffn(call, fs, make_tile, out_write):
            zcol = fs["zcol"]
            xt_f = fs["xt_f"]; xb_t = fs["xb_t"]; sq_t = fs["sq_t"]
            fmsc = fs["fmsc"]; logit_t = fs["logit_t"]; oh_t = fs["oh_t"]
            msk_t = fs["msk_t"]; nrmax = fs["nrmax"]; sume = fs["sume"]
            gcol = fs["gcol"]; ecol = fs["ecol"]; mu_c = fs["mu_c"]
            rho_c = fs["rho_c"]; eidx_row = fs["eidx_row"]
            ohT = fs["ohT"]; crun = fs["crun"]; comb_s = fs["comb_s"]
            rowscr = fs["rowscr"]; slot_c = fs["slot_c"]; posg_c = fs["posg_c"]
            gate_c = fs["gate_c"]; acc_c = fs["acc_c"]; gw_c = fs["gw_c"]
            slot_u = fs["slot_u"]; gret_u = fs["gret_u"]
            gfin_c = fs["gfin_c"]; idxcol = fs["idxcol"]
            cnt84 = fs["cnt84"]; m48a = fs["m48a"]
            m48b = fs["m48b"]; m48c = fs["m48c"]; m84a = fs["m84a"]; m84b = fs["m84b"]
            cnteT = fs["cnteT"]; cntd_sb = fs["cntd_sb"]; thr_col = fs["thr_col"]
            dlt_col = fs["dlt_col"]; idx4 = fs["idx4"]; cntrow = fs["cntrow"]
            basem1 = fs["basem1"]
            gbuf = fs["gbuf"]
            xe_sb = fs["xe_sb"]; wg_sb = fs["wg_sb"]; wu_sb = fs["wu_sb"]
            wd_sb = fs["wd_sb"]; a_sb = fs["a_sb"]; sg_sc = fs["sg_sc"]
            y_sb = fs["y_sb"]; yb_t = fs["yb_t"]; yf_t = fs["yf_t"]
            xbf_t = fs["xbf_t"]; r_t = fs["r_t"]

            tk_zc = ve.op(lambda e: e.memset(zcol[:], 0.0))

            tok_xb_dma = sq.tok()
            tok_fm_evac = ve.tok()
            tok_router_read = ve.tok()
            for i in range(NT):
                ttile = make_tile(i)
                t1 = ve.op(lambda e: e.tensor_reduce(st1[:], xt_f[:], axis=AX.X,
                                                     op=ALU.add), deps=[ttile])
                t2 = sc.op(lambda e: e.activation(sq_t[:], xt_f[:], AF.Square,
                                                  accum_out=st2[:, 0:1]), deps=[ttile])
                ve.op(lambda e, i=i: e.tensor_scalar(mu_c[:, i:i + 1], st1[:],
                                                     1.0 / H, None, ALU.mult),
                      deps=[t1])
                ve.op(lambda e, i=i: e.tensor_tensor(st3[:], mu_c[:, i:i + 1],
                                                     mu_c[:, i:i + 1], ALU.mult))
                ve.op(lambda e: e.scalar_tensor_tensor(st3[:], st2[:], 1.0 / H,
                                                       st3[:], ALU.mult,
                                                       ALU.subtract), deps=[t2])
                ve.op(lambda e: e.tensor_scalar(st3[:], st3[:], LN_EPS, None, ALU.add))
                ve.op(lambda e: e.reciprocal(st4[:], st3[:]))
                trho = sc.op(lambda e, i=i: e.activation(rho_c[:, i:i + 1], st4[:],
                                                         AF.Sqrt), deps=[ve.tok()])
                tstg = ve.op(lambda e, i=i: e.tensor_scalar(
                    xb_t[:], xt_f[:], mu_c[:, i:i + 1], rho_c[:, i:i + 1],
                    ALU.subtract, ALU.mult), deps=[trho, tok_xb_dma])
                tok_xb_dma = sq.dma(xln_dr[i * 128:(i + 1) * 128, :], xb_t[:],
                                    deps=[tstg])
                for c in range(NCH):
                    pe.op(lambda e, c=c: e.transpose(
                        ps_y[:, c * 128:(c + 1) * 128],
                        xt_f[:, c * 128:(c + 1) * 128], identt[:]),
                        deps=([ttile, tok_fm_evac, tk_id] if c == 0 else ()),
                        inc=(c == NCH - 1))
                ttr = pe.tok()
                tok_fm_evac = ve.op(lambda e: e.tensor_copy(fmsc[:], ps_y[:, :]),
                                    deps=[ttr])
                for c in range(NCH):
                    pe.op(lambda e, c=c: e.matmul(
                        ps_c[:, 0:NE], fmsc[:, c * 128:(c + 1) * 128],
                        wr_sb[:, c * NE:(c + 1) * NE],
                        start=(c == 0), stop=(c == NCH - 1)),
                        deps=([tok_fm_evac, tok_router_read, tk_setup]
                              if c == 0 else ()),
                        inc=(c == NCH - 1))
                traw = pe.tok()
                t3 = ve.op(lambda e, i=i: e.scalar_tensor_tensor(
                    logit_t[:], csum_b[:], mu_c[:, i:i + 1], ps_c[:, 0:NE],
                    ALU.mult, ALU.subtract), deps=[traw])
                tok_router_read = t3
                t4 = ve.op(lambda e, i=i: e.tensor_scalar(st3[:], rho_c[:, i:i + 1],
                                                          -1.0, None, ALU.mult))
                t5 = ve.op(lambda e: e.tensor_scalar(logit_t[:], logit_t[:],
                                                     st3[:, 0:1], None, ALU.mult),
                           deps=[t3, t4])
                t6 = ve.op(lambda e: e.reduce_max(nrmax[:], logit_t[:], axis=AX.X,
                                                  negate=True), deps=[t5])
                t7 = sc.op(lambda e: e.activation(msk_t[:], logit_t[:], AF.Exp,
                                                  bias=nrmax[:, 0:1], scale=1.0,
                                                  accum_out=sume[:, 0:1]), deps=[t6])
                ve.op(lambda e: e.reciprocal(gcol[:], sume[:]), deps=[t7])
                t8 = ve.op(lambda e: e.tensor_scalar(oh_t[:], logit_t[:],
                                                     nrmax[:, 0:1], 0.0,
                                                     ALU.add, ALU.is_ge), deps=[t6])
                t9 = ve.op(lambda e: e.scalar_tensor_tensor(
                    msk_t[:], oh_t[:], -1e6, iotaBt[:], ALU.mult, ALU.add), deps=[t8])
                tec = ve.op(lambda e: e.tensor_reduce(ecol[:], msk_t[:], axis=AX.X,
                                                      op=ALU.min), deps=[t9])
                ve.op(lambda e, i=i: e.tensor_copy(gate_c[:, i:i + 1], gcol[:]))
                pt1 = pe.op(lambda e: e.transpose(ps_a[0:1, 0:128], ecol[:, 0:1],
                                                  identt[:]), deps=[tec], inc=True)
                tev = ve.op(lambda e, i=i: e.tensor_copy(
                    eidx_row[0:1, (i % 4) * 128:(i % 4 + 1) * 128],
                    ps_a[0:1, 0:128]), deps=[pt1])
                if i % 4 == 3:
                    s = i // 4
                    pm = pe.op(lambda e: e.matmul(
                        ps_b[0:32, 0:512], ones32r[0:1, :], eidx_row[0:1, :],
                        start=True, stop=True), deps=[tev], inc=True)
                    ve.op(lambda e, s=s: e.tensor_scalar(
                        ohT[:, s * 512:(s + 1) * 512], ps_b[0:32, 0:512],
                        iota32c[:, 0:1], None, ALU.is_equal), deps=[pm])

            tk_rows = ve.tok()

            tprev = tk_rows
            tscan = ve.op(lambda e: e.tensor_tensor_scan(
                crun[:, :], ohT[:, :], zcol[0:32, 0:1].to_broadcast([32, T]),
                0.0, ALU.add, ALU.add), deps=[tprev, tk_zc])
            tcnt = ve.op(lambda e: e.tensor_copy(fs["cnt32"][:],
                                                 crun[:, T - 1:T]), deps=[tscan])
            ptc = pe.op(lambda e: e.transpose(ps_a[0:1, 0:32], fs["cnt32"][:, 0:1],
                                              identt[0:32, 0:32]), deps=[tcnt],
                        inc=True)
            tcr = ve.op(lambda e: e.tensor_copy(cntrow[:], ps_a[0:1, 0:32]),
                        deps=[ptc])
            tdma_cnt = sq.dma(cnt_src[0:1, :], cntrow[0:1, :], deps=[tcr, sq.tok()])
            cc_ag = k.collective("AllGather", [cnt_src[:, :].opt()],
                                 [cnt_dst[:, :].opt()], deps=[tdma_cnt])
            tb1 = sq.dma(b32a[:], cntrow[0:1, :], deps=[tcr, sq.tok()])
            tb2 = sq.dma(cnt84[:], b32a.rearrange("(d j) -> d j", d=8), deps=[tb1, sq.tok()])
            tcum = ve.op(lambda e: e.tensor_tensor_scan(
                m84a[:], cnt84[:], zcol[0:8, 0:1].to_broadcast([8, 4]),
                0.0, ALU.add, ALU.add), deps=[tb2])
            ve.op(lambda e: e.tensor_sub(m84a[:], m84a[:], cnt84[:]))
            tBm = ve.op(lambda e: e.tensor_add(m84a[:], m84a[:], dstb84m1[:]))
            tb3 = sq.dma(b32b[:], m84a[:, :], deps=[tBm, sq.tok()])
            tb4 = sq.dma(fs["Bm1"][:, :], b32b[:], deps=[tb3, sq.tok()])
            def strip_cols(col_ap, dst, deps0):
                tp = ve.tok()
                for s in range(8):
                    tstt = ve.op(lambda e, s=s: e.scalar_tensor_tensor(
                        comb_s[:, :], crun[:, s * 512:(s + 1) * 512], col_ap,
                        ohT[:, s * 512:(s + 1) * 512], ALU.add, ALU.mult),
                        deps=list(deps0) + [tp])
                    pm = pe.op(lambda e: e.matmul(
                        ps_a[0:1, 0:512], ones32c[:, 0:1], comb_s[:, :],
                        start=True, stop=True), deps=[tstt], inc=True)
                    trow = ve.op(lambda e: e.tensor_copy(rowscr[:, :],
                                                         ps_a[0:1, 0:512]),
                                 deps=[pm])
                    for q in range(4):
                        pe.op(lambda e, q=q: e.transpose(
                            ps_b[0:128, q:q + 1],
                            rowscr[0:1, q * 128:(q + 1) * 128],
                            identt[0:1, 0:1]),
                            deps=([trow] if q == 0 else ()), inc=(q == 3))
                    ptr = pe.tok()
                    tp = ve.op(lambda e, s=s: e.tensor_copy(
                        dst[:, s * 4:(s + 1) * 4], ps_b[0:128, 0:4]), deps=[ptr])
                return tp
            tprev2 = strip_cols(fs["Bm1"][:, 0:1], slot_c, [tb4, tscan])
            if DBG and call == 1:
                sq.dma(dbg_strip[:, 0:512], comb_s[:, :], deps=[tprev2])
                sq.dma(dbg_strip[0:1, 512:1024], rowscr[0:1, :], deps=[tprev2])
                sq.dma(dbg_strip[:, 1024:1025], fs["Bm1"][:, :], deps=[tprev2],
                       allow_slow_non_contiguous=True)
            tld = sq.dma(cntd_sb[:], cnt_dst[:, :], deps=[cc_ag, sq.tok()])
            tmask = ve.op(lambda e: e.tensor_scalar(
                fs["cntd_m"][:], cntd_sb[:], mprev[:, 0:1], None, ALU.mult),
                deps=[tld])
            pbm = pe.op(lambda e: e.matmul(ps_b[0:32, 0:1], fs["cntd_m"][:],
                                           ones8c[:, 0:1], start=True, stop=True),
                        deps=[tmask], inc=True)
            tbase = ve.op(lambda e: e.tensor_scalar(basem1[:], ps_b[0:32, 0:1],
                                                    1.0, None, ALU.subtract),
                          deps=[pbm])
            tprev3 = strip_cols(basem1[:, 0:1], posg_c, [tbase, tprev2])
            tacc = ve.op(lambda e: e.tensor_scalar(
                acc_c[:], posg_c[:], float(CAP), None, ALU.is_lt), deps=[tprev3])
            ve.op(lambda e: e.scalar_tensor_tensor(
                gfin_c[:], gate_c[:], 0.5, acc_c[:], ALU.mult, ALU.mult),
                deps=[tacc])
            ve.op(lambda e: e.tensor_scalar(gw_c[:], slot_c[:], ZROW, None,
                                            ALU.subtract), deps=[tprev2])
            ve.op(lambda e: e.tensor_mul(gw_c[:], gw_c[:], acc_c[:]))
            ve.op(lambda e: e.tensor_scalar(gw_c[:], gw_c[:], ZROW, None, ALU.add))
            ve.op(lambda e: e.tensor_copy(gret_u[:], gw_c[:]))
            ve.op(lambda e: e.tensor_copy(slot_u[:], slot_c[:]))
            tk_cols = ve.tok()

            tok_scat = gp.tok()
            for i in range(NT):
                tld = sq.dma(xb_t[:], xln_dr[i * 128:(i + 1) * 128, :],
                             deps=[tok_xb_dma, tok_scat, ve.tok()])
                tok_scat = gp.iscatter(
                    send_dr[:, :], slot_u[:, i:i + 1], xb_t[:],
                    deps=[tk_cols, tld, tk_zbufs],
                    bounds_check=k.bc_reg, oob_is_err=False)
            cc_send = k.collective("AllToAll", [send_dr[:, :].opt()],
                                   [recv_dr[0:AR, :].opt()], deps=[tok_scat])

            tE = sq.dma(cnteT[:], cnt_dst.rearrange("s e -> e s"),
                        deps=[cc_ag, sq.tok()],
                        allow_slow_non_contiguous=True)
            p1 = pe.op(lambda e: e.matmul(ps_a[0:4, 0:8], osel[:], cnteT[:],
                                          start=True, stop=True),
                       deps=[tE, tk_t0], inc=True)
            tm48 = ve.op(lambda e: e.tensor_copy(m48a[:], ps_a[0:4, 0:8]), deps=[p1])
            p2 = pe.op(lambda e: e.matmul(ps_b[0:8, 0:4], cnteT[:], osel[:],
                                          start=True, stop=True), deps=[tE],
                       inc=True)
            tm84 = ve.op(lambda e: e.tensor_copy(m84b[:], ps_b[0:8, 0:4]), deps=[p2])
            tc48 = ve.op(lambda e: e.tensor_tensor_scan(
                m48b[:], m48a[:], zcol[0:4, 0:1].to_broadcast([4, 8]),
                0.0, ALU.add, ALU.add), deps=[tm48])
            tcx48 = ve.op(lambda e: e.tensor_sub(m48b[:], m48b[:], m48a[:]),
                          deps=[tc48])
            tc84 = ve.op(lambda e: e.tensor_tensor_scan(
                cnt84[:], m84b[:], zcol[0:8, 0:1].to_broadcast([8, 4]),
                0.0, ALU.add, ALU.add), deps=[tm84])
            tso84 = ve.op(lambda e: e.tensor_sub(cnt84[:], cnt84[:], m84b[:]),
                          deps=[tc84])
            p3 = pe.op(lambda e: e.transpose(ps_c[0:4, 0:8], cnt84[0:8, 0:4],
                                             identt[0:8, 0:8]), deps=[tso84],
                       inc=True)
            tsoT = ve.op(lambda e: e.tensor_copy(m48c[:], ps_c[0:4, 0:8]), deps=[p3])
            td1 = ve.op(lambda e: e.tensor_add(m48c[:], m48c[:], sbase48[:]),
                        deps=[tsoT])
            td2 = ve.op(lambda e: e.tensor_sub(m48c[:], m48c[:], m48b[:]),
                        deps=[td1, tcx48])
            td3 = ve.op(lambda e: e.tensor_copy(m48a[:, 0:1], m48c[:, 0:1]),
                        deps=[td2])
            td4 = ve.op(lambda e: e.tensor_sub(m48a[:, 1:8], m48c[:, 1:8],
                                               m48c[:, 0:7]), deps=[td3])
            tb5 = sq.dma(b32b[:], m48b[:, :], deps=[td4, sq.tok()])
            tb6 = sq.dma(thr_col[:, :], b32b[:], deps=[tb5, sq.tok()])
            tb7 = sq.dma(b32c[:], m48a[:, :], deps=[td4, sq.tok()])
            tb8 = sq.dma(dlt_col[:, :], b32c[:], deps=[tb7, sq.tok()])
            tmj = ve.op(lambda e: e.tensor_scalar(
                fs["maskJ"][:, :], iotaJt[:], thr_col[:, 0:1], None, ALU.is_ge),
                deps=[tb6, tk_t0])
            tmj2 = ve.op(lambda e: e.tensor_scalar(
                fs["maskJ"][:, :], fs["maskJ"][:, :], dlt_col[:, 0:1], None, ALU.mult),
                deps=[tb8])
            tprev4 = tmj2
            for (o, n) in BLK:
                pm = pe.op(lambda e, o=o, n=n: e.matmul(
                    ps_d[0:4, 0:n], bselt[:], fs["maskJ"][:, o:o + n],
                    start=True, stop=True), deps=[tprev4], inc=True)
                tprev4 = ve.op(lambda e, o=o, n=n: e.tensor_tensor(
                    idx4[0:4, o:o + n], ps_d[0:4, 0:n], iotaJt[0:4, o:o + n],
                    ALU.add), deps=[pm])
            tclamp = ve.op(lambda e: e.tensor_scalar_min(idx4[:, :], idx4[:, :], ZROW),
                           deps=[tprev4])
            tok_idxev = ve.tok()
            for g in range(NG):
                pt = pe.op(lambda e, g=g: e.transpose(
                    ps_a[0:128, 0:4], idx4[0:4, g * 128:(g + 1) * 128],
                    identt[0:4, 0:4]), deps=[tclamp, tok_idxev], inc=True)
                tok_idxev = ve.op(lambda e, g=g: e.tensor_copy(
                    idxcol[:, g * EPC:(g + 1) * EPC], ps_a[:, 0:4]), deps=[pt])
            tk_idxcol = ve.tok()

            if DBG and call == 1:
                dtk = ve.tok()
                sq.dma(dbg_cols[:, 0 * NT:1 * NT], slot_c[:], deps=[tk_cols])
                sq.dma(dbg_cols[:, 1 * NT:2 * NT], posg_c[:], deps=[tk_cols])
                sq.dma(dbg_cols[:, 2 * NT:3 * NT], gate_c[:], deps=[tk_cols])
                sq.dma(dbg_cols[:, 3 * NT:4 * NT], gfin_c[:], deps=[tk_cols])
                sq.dma(dbg_cols[:, 4 * NT:5 * NT], acc_c[:], deps=[tk_cols])
                sq.dma(dbg_idx[:, :], idxcol[:], deps=[tk_idxcol])
                sq.dma(dbg_cnt[:, :], cntd_sb[:], deps=[tk_idxcol])
                sq.dma(dbg_sm[:, 0:1], basem1[:, :], deps=[tk_idxcol],
                       allow_slow_non_contiguous=True)
                sq.dma(dbg_sm[:, 1:2], fs["Bm1"][:, :], deps=[tk_idxcol],
                       allow_slow_non_contiguous=True)
                sq.dma(dbg_sm[:, 2:3], thr_col[:, :], deps=[tk_idxcol],
                       allow_slow_non_contiguous=True)
                sq.dma(dbg_sm[:, 3:4], dlt_col[:, :], deps=[tk_idxcol],
                       allow_slow_non_contiguous=True)
                sq.dma(dbg_sm[:, 4:12], cnteT[:, :], deps=[tk_idxcol],
                       allow_slow_non_contiguous=True)
                sq.dma(dbg_sm[0:4, 12:20], m48a[:, :], deps=[tk_idxcol],
                       allow_slow_non_contiguous=True)
                sq.dma(dbg_sm[0:1, 20:24], fs["cnt32"][0:1, 0:1], deps=[tk_idxcol],
                       allow_slow_non_contiguous=True) if False else None

            tok_wload = {"g": sq.tok(), "u": sq.tok(), "d": sq.tok()}
            tok_xe_loads = [sq.tok(), sq.tok()]
            gbuf_used = sq.tok()
            tok_xe_used = pe.tok()
            tok_wgu_used = pe.tok()
            tok_wd_used = pe.tok()
            tok_a_used = pe.tok()
            psg = (ps_a, ps_b)
            psu = (ps_c, ps_d)
            psy = (ps_y, ps_w)
            tok_g_evac = [sc.tok(), sc.tok()]
            tok_u_evac = [ve.tok(), ve.tok()]
            tok_y_scat = [gp.tok(), gp.tok()]
            tok_y_evac = [sc.tok(), sc.tok()]

            for e_i in range(EPC):
                par = e_i % 2
                gts = []
                for g in range(NG):
                    tgb = gp.igather(
                        gbuf[:], recv_dr[:, :],
                        idxcol[:, g * EPC + e_i:g * EPC + e_i + 1],
                        deps=[cc_send, tk_idxcol, tok_xe_loads[par], gbuf_used])
                    gbuf_used = sq.dma(gath_dr[par, g * 128:(g + 1) * 128, :],
                                       gbuf[:], deps=[tgb])
                    gts.append(gbuf_used)
                xe_toks = []
                for c in range(NCH):
                    xe_toks.append(sq.dmat(
                        xe_sb[:, c * CAP:(c + 1) * CAP],
                        gath_dr[par, :, c * 128:(c + 1) * 128],
                        deps=[gts[-1], tok_xe_used]))
                tk_xe = xe_toks[-1]
                tok_xe_loads[par] = tk_xe
                tok_wload["g"] = gp.dma(
                    wg_sb[:, :].rearrange("p (c f) -> p c f", c=NCH),
                    wg_in[e_i].rearrange("(c p) f -> p c f", p=128),
                    deps=[tok_wgu_used])
                tok_wload["u"] = gp.dma(
                    wu_sb[:, :].rearrange("p (c f) -> p c f", c=NCH),
                    wu_in[e_i].rearrange("(c p) f -> p c f", p=128),
                    deps=[tok_wgu_used])
                tok_wload["d"] = gp.dma(
                    wd_sb[:, :].rearrange("p (c f) -> p c f", c=NCH),
                    wd_in[e_i].rearrange("(c p) f -> p c f", p=128),
                    deps=[tok_wd_used])
                for ft in range(NCH):
                    for bi, (o, n) in enumerate(BLK):
                        pp = (ft * 3 + bi) % 2
                        for c in range(NCH):
                            pe.op(lambda e, ft=ft, o=o, n=n, c=c, pp=pp: e.matmul(
                                psg[pp][:, 0:n],
                                wg_sb[:, c * 1024 + ft * 128:c * 1024 + (ft + 1) * 128],
                                xe_sb[:, c * CAP + o:c * CAP + o + n],
                                start=(c == 0), stop=(c == NCH - 1)),
                                deps=([tk_xe, tok_wload["g"], tok_g_evac[pp]]
                                      if c == 0 else ()),
                                inc=(c == NCH - 1))
                        tg = pe.tok()
                        for c in range(NCH):
                            pe.op(lambda e, ft=ft, o=o, n=n, c=c, pp=pp: e.matmul(
                                psu[pp][:, 0:n],
                                wu_sb[:, c * 1024 + ft * 128:c * 1024 + (ft + 1) * 128],
                                xe_sb[:, c * CAP + o:c * CAP + o + n],
                                start=(c == 0), stop=(c == NCH - 1)),
                                deps=([tok_wload["u"], tok_u_evac[pp]]
                                      if c == 0 else ()),
                                inc=(c == NCH - 1))
                        tu = pe.tok()
                        tok_g_evac[pp] = sc.op(lambda e, n=n, pp=pp: e.activation(
                            sg_sc[pp][:, 0:n], psg[pp][:, 0:n], AF.Silu),
                            deps=[tg, tok_u_evac[pp]])
                        tok_u_evac[pp] = ve.op(
                            lambda e, ft=ft, o=o, n=n, pp=pp: e.tensor_tensor(
                                a_sb[:, ft * CAP + o:ft * CAP + o + n],
                                sg_sc[pp][:, 0:n], psu[pp][:, 0:n], ALU.mult),
                            deps=[tok_g_evac[pp], tu])
                tok_wgu_used = pe.tok()
                tk_a = ve.tok()
                for tt in range(NG):
                    pp = tt % 2
                    for half in range(2):
                        for fc in range(NCH):
                            pe.op(lambda e, tt=tt, half=half, fc=fc, pp=pp: e.matmul(
                                psy[pp][:, half * 512:(half + 1) * 512],
                                a_sb[:, fc * CAP + tt * 128:fc * CAP + (tt + 1) * 128],
                                wd_sb[:, fc * 1024 + half * 512:fc * 1024 + (half + 1) * 512],
                                start=(fc == 0), stop=(fc == NCH - 1)),
                                deps=([tk_a, tok_wload["d"], tok_y_evac[pp]]
                                      if (fc == 0 and half == 0) else ()),
                                inc=(fc == NCH - 1 and half == 1))
                    ty = pe.tok()
                    tok_y_evac[pp] = sc.op(lambda e, pp=pp: e.copy(
                        y_sb[pp][:], psy[pp][:, :]), deps=[ty, tok_y_scat[pp]])
                    tok_y_scat[pp] = gp.iscatter(
                        rsrc_dr[:, :], idxcol[:, tt * EPC + e_i:tt * EPC + e_i + 1],
                        y_sb[pp][:], deps=[tok_y_evac[pp]])
                tok_xe_used = pe.tok()
                tok_wd_used = pe.tok()
                tok_a_used = pe.tok()
            cc_ret = k.collective("AllToAll", [rsrc_dr[0:AR, :].opt()],
                                  [rbuf_dr[0:AR, :].opt()],
                                  deps=[tok_y_scat[0], tok_y_scat[1]])

            tok_y_used = ve.tok()
            tok_out = sq.tok()
            for i in range(NT):
                tgy = gp.igather(yb_t[:], rbuf_dr[:, :], gret_u[:, i:i + 1],
                                 deps=[cc_ret, tok_y_used])
                tyf = ve.op(lambda e: e.tensor_copy(yf_t[:], yb_t[:]), deps=[tgy])
                tok_y_used = tyf
                tys = ve.op(lambda e, i=i: e.tensor_scalar(
                    yf_t[:], yf_t[:], gfin_c[:, i:i + 1], None, ALU.mult),
                    deps=[tyf])
                txl = sq.dma(xb_t[:], xln_dr[i * 128:(i + 1) * 128, :],
                             deps=[ve.tok(), tok_xb_dma])
                txf = ve.op(lambda e: e.tensor_copy(xbf_t[:], xb_t[:]), deps=[txl])
                trt = out_write(i, txf, tys)
            return

        def make_ffn_scope(ctx, tag=""):
            fs = {}
            fs["zcol"] = sb(tag + "f_zc", [128, 1], ctx=ctx)
            fs["xt_f"] = sb(tag + "f_xt", [128, H], ctx=ctx)
            fs["xb_t"] = sb(tag + "f_xb", [128, H], BF16, ctx=ctx)
            fs["sq_t"] = sb(tag + "f_sq", [128, H], ctx=ctx)
            fs["fmsc"] = sb(tag + "f_fm", [128, H], ctx=ctx)
            fs["logit_t"] = sb(tag + "f_lg", [128, NE], ctx=ctx)
            fs["oh_t"] = sb(tag + "f_oh", [128, NE], ctx=ctx)
            fs["msk_t"] = sb(tag + "f_mk", [128, NE], ctx=ctx)
            fs["nrmax"] = sb(tag + "f_nm", [128, 1], ctx=ctx)
            fs["sume"] = sb(tag + "f_se", [128, 1], ctx=ctx)
            fs["gcol"] = sb(tag + "f_gc", [128, 1], ctx=ctx)
            fs["ecol"] = sb(tag + "f_ec", [128, 1], ctx=ctx)
            fs["mu_c"] = sb(tag + "f_mu", [128, NT], ctx=ctx)
            fs["rho_c"] = sb(tag + "f_rh", [128, NT], ctx=ctx)
            fs["ohT"] = sb(tag + "f_ohT", [NE, T], ctx=ctx)
            fs["crun"] = sb(tag + "f_crun", [NE, T], ctx=ctx)
            fs["comb_s"] = sb(tag + "f_cmbs", [NE, 512], ctx=ctx)
            fs["eidx_row"] = sb(tag + "f_er", [1, 512], ctx=ctx)
            fs["rowscr"] = sb(tag + "f_rsc", [1, 512], ctx=ctx)
            fs["slot_c"] = sb(tag + "f_slc", [128, NT], ctx=ctx)
            fs["posg_c"] = sb(tag + "f_pgc", [128, NT], ctx=ctx)
            fs["gate_c"] = sb(tag + "f_gtc", [128, NT], ctx=ctx)
            fs["acc_c"] = sb(tag + "f_acc", [128, NT], ctx=ctx)
            fs["gw_c"] = sb(tag + "f_gwc", [128, NT], ctx=ctx)
            fs["slot_u"] = sb(tag + "f_su", [128, NT], U32, ctx=ctx)
            fs["gret_u"] = sb(tag + "f_gu", [128, NT], U32, ctx=ctx)
            fs["gfin_c"] = sb(tag + "f_gfc", [128, NT], ctx=ctx)
            fs["idxcol"] = sb(tag + "f_ic", [128, NG * EPC], U32, ctx=ctx)
            fs["cnt32"] = sb(tag + "f_c32", [NE, 1], ctx=ctx)
            fs["cntrow"] = sb(tag + "f_crow", [1, NE], ctx=ctx)
            fs["cnt84"] = sb(tag + "f_c84", [8, 4], ctx=ctx)
            fs["m48a"] = sb(tag + "f_m48a", [4, 8], ctx=ctx)
            fs["m48b"] = sb(tag + "f_m48b", [4, 8], ctx=ctx)
            fs["m48c"] = sb(tag + "f_m48c", [4, 8], ctx=ctx)
            fs["m84a"] = sb(tag + "f_m84a", [8, 4], ctx=ctx)
            fs["m84b"] = sb(tag + "f_m84b", [8, 4], ctx=ctx)
            fs["cnteT"] = sb(tag + "f_cT", [NE, 8], ctx=ctx)
            fs["cntd_sb"] = sb(tag + "f_cd", [8, NE], ctx=ctx)
            fs["cntd_m"] = sb(tag + "f_cdm", [8, NE], ctx=ctx)
            fs["thr_col"] = sb(tag + "f_th", [NE, 1], ctx=ctx)
            fs["dlt_col"] = sb(tag + "f_dl", [NE, 1], ctx=ctx)
            fs["basem1"] = sb(tag + "f_bm", [NE, 1], ctx=ctx)
            fs["Bm1"] = sb(tag + "f_Bm1", [NE, 1], ctx=ctx)
            fs["maskJ"] = sb(tag + "f_mJ", [NE, CAP], ctx=ctx)
            fs["idx4"] = sb(tag + "f_i4", [4, CAP], ctx=ctx)
            fs["xe_sb"] = sb(tag + "f_xe", [128, NCH * CAP], BF16, ctx=ctx)
            fs["wg_sb"] = sb(tag + "f_wg", [128, NCH * 1024], BF16, ctx=ctx)
            fs["wu_sb"] = sb(tag + "f_wu", [128, NCH * 1024], BF16, ctx=ctx)
            fs["wd_sb"] = sb(tag + "f_wd", [128, NCH * 1024], BF16, ctx=ctx)
            fs["a_sb"] = sb(tag + "f_a", [128, NCH * CAP], BF16, ctx=ctx)
            fs["sg_sc"] = (sb(tag + "f_sg0", [128, 512], ctx=ctx),
                           sb(tag + "f_sg1", [128, 512], ctx=ctx))
            fs["y_sb"] = (sb(tag + "f_y0", [128, 1024], BF16, ctx=ctx),
                          sb(tag + "f_y1", [128, 1024], BF16, ctx=ctx))
            fs["gbuf"] = sb(tag + "f_gbuf", [128, H], BF16, ctx=ctx)
            fs["yb_t"] = sb(tag + "f_ybt", [128, H], BF16, ctx=ctx)
            fs["yf_t"] = sb(tag + "f_yft", [128, H], ctx=ctx)
            fs["xbf_t"] = sb(tag + "f_xbf", [128, H], ctx=ctx)
            fs["r_t"] = sb(tag + "f_rt", [128, H], ctx=ctx)
            return fs

        with ExitStack() as c1:
            fs = make_ffn_scope(c1, "c1")
            tok_r2_dma = sq.tok()

            def make_tile1(i, fs=fs):
                return sq.dma(fs["xt_f"][:], x_in[i * 128:(i + 1) * 128, :],
                              deps=[ve.tok(), sc.tok(), pe.tok()])


            def out_write1(i, txf, tys, fs=fs):
                nonlocal tok_r2_dma
                t1 = ve.op(lambda e: e.scalar_tensor_tensor(
                    fs["r_t"][:], fs["xbf_t"][:], 0.5, fs["xt_f"][:],
                    ALU.mult, ALU.add), deps=[txf, _mk1_load[i], tok_r2_dma])
                t2 = ve.op(lambda e: e.tensor_add(fs["r_t"][:], fs["r_t"][:],
                                                  fs["yf_t"][:]), deps=[tys])
                t3 = ve.op(lambda e: e.tensor_reduce(st1[:], fs["r_t"][:],
                                                     axis=AX.X, op=ALU.add))
                t4 = sc.op(lambda e: e.activation(fs["sq_t"][:], fs["r_t"][:],
                                                  AF.Square, accum_out=st2[:, 0:1]),
                           deps=[t2])
                ve.op(lambda e, i=i: e.tensor_scalar(mu2_c[:, i:i + 1], st1[:],
                                                     1.0 / H, None, ALU.mult),
                      deps=[t3])
                ve.op(lambda e, i=i: e.tensor_tensor(st3[:], mu2_c[:, i:i + 1],
                                                     mu2_c[:, i:i + 1], ALU.mult))
                ve.op(lambda e: e.scalar_tensor_tensor(st3[:], st2[:], 1.0 / H,
                                                       st3[:], ALU.mult,
                                                       ALU.subtract), deps=[t4])
                ve.op(lambda e: e.tensor_scalar(st3[:], st3[:], LN_EPS, None,
                                                ALU.add))
                ve.op(lambda e: e.reciprocal(st4[:], st3[:]))
                sc.op(lambda e, i=i: e.activation(rho2_c[:, i:i + 1], st4[:],
                                                  AF.Sqrt), deps=[ve.tok()])
                tok_r2_dma = sq.dma(r2_dr[i * 128:(i + 1) * 128, :], fs["r_t"][:],
                                    deps=[t2])
                if stage == 1:
                    tok_r2_dma = sq.dma(out_dr[i * 128:(i + 1) * 128, :],
                                        fs["r_t"][:], deps=[t2])
                return tok_r2_dma

            _mk1_load = {}

            def make_tile1_full(i):
                t = make_tile1(i)
                return t

            def out_write1_wrap(i, txf, tys):
                _mk1_load[i] = sq.dma(fs["xt_f"][:],
                                      x_in[i * 128:(i + 1) * 128, :],
                                      deps=[ve.tok(), sc.tok(), pe.tok()])
                return out_write1(i, txf, tys)

            emit_ffn(1, fs, make_tile1_full, out_write1_wrap)
            k.barrier()

        if stage >= 2:
            r2fm_dr = nc.dram_tensor("r2fm_dr", [H, T], F32, kind="Internal").ap()
            with ExitStack() as ca:
                r2fm = sb("a_r2fm", [128, T], ctx=ca)
                fmtile = sb("a_fmt", [128, H], ctx=ca)
                sA = sb("a_sA", [128, T], ctx=ca)
                l0b = sb("a_l0", [128, 2048], ctx=ca)
                h0b = sb("a_h0", [128, 2048], ctx=ca)
                lo2 = sb("a_lo2", [128, 1024], ctx=ca)
                h1b = sb("a_h1", [128, 1024], ctx=ca)
                qb = sb("a_qb", [128, 1024], ctx=ca)
                qT_sb = sb("a_qT", [128, 128], ctx=ca)
                vT_sb = sb("a_vT", [128, 128], ctx=ca)
                kv_sb = sb("a_kv", [128, 64], ctx=ca)
                num_sb = sb("a_num", [128, 1024], ctx=ca)
                den_sb = sb("a_den", [1, 2048], ctx=ca)
                mr2_row = sb("a_mrr", [1, T], ctx=ca)
                rho2_row = sb("a_rhr", [1, T], ctx=ca)
                mr_c = sb("a_mrc", [128, NT], ctx=ca)
                rt_l = sb("a_rt", [128, H], ctx=ca)

                tmr = ve.op(lambda e: e.tensor_mul(mr_c[:], mu2_c[:], rho2_c[:]))
                tok_ev = ve.tok()
                for i in range(NT):
                    pt = pe.op(lambda e, i=i: e.transpose(
                        ps_a[0:1, 0:128], mr_c[:, i:i + 1], identt[:]),
                        deps=[tmr, tok_ev], inc=True)
                    ve.op(lambda e, i=i: e.tensor_copy(
                        mr2_row[0:1, i * 128:(i + 1) * 128], ps_a[0:1, 0:128]),
                        deps=[pt])
                    pt2 = pe.op(lambda e, i=i: e.transpose(
                        ps_a[0:1, 128:256], rho2_c[:, i:i + 1], identt[:]),
                        deps=[ve.tok()], inc=True)
                    tok_ev = ve.op(lambda e, i=i: e.tensor_copy(
                        rho2_row[0:1, i * 128:(i + 1) * 128], ps_a[0:1, 128:256]),
                        deps=[pt2])

                tok_ld = sq.tok()
                tok_ev2 = ve.tok()
                tok_fmdma = sq.tok()
                for i in range(NT):
                    tok_ld = sq.dma(rt_l[:], r2_dr[i * 128:(i + 1) * 128, :],
                                    deps=[tok_ev2])
                    for c in range(NCH):
                        pe.op(lambda e, c=c: e.transpose(
                            ps_y[:, c * 128:(c + 1) * 128],
                            rt_l[:, c * 128:(c + 1) * 128], identt[:]),
                            deps=([tok_ld, tok_ev2] if c == 0 else ()),
                            inc=(c == NCH - 1))
                    ptr = pe.tok()
                    tok_ev2 = ve.op(lambda e: e.tensor_copy(fmtile[:], ps_y[:, :]),
                                    deps=[ptr, tok_fmdma])
                    tok_fmdma = sq.dma(
                        r2fm_dr[:, i * 128:(i + 1) * 128]
                        .rearrange("(c p) t -> p c t", p=128),
                        fmtile[:, :].rearrange("p (c t) -> p c t", c=NCH),
                        deps=[tok_ev2])

                def fcol(fi, c, kk):
                    return fil[:, (fi * NCH + c) * 4 + kk:(fi * NCH + c) * 4 + kk + 1]

                def scol(j, c, kk):
                    return fsyn[:, (j * NCH + c) * 3 + kk:(j * NCH + c) * 3 + kk + 1]

                def conv_a(dst, src, L, fi, c, ch, accum=False):
                    Lo = L // 2
                    t = ch.op(lambda e: (e.scalar_tensor_tensor(
                        dst[:, 0:Lo], src[:, 0:L:2], fcol(fi, c, 1), dst[:, 0:Lo],
                        ALU.mult, ALU.add) if accum else e.tensor_scalar(
                        dst[:, 0:Lo], src[:, 0:L:2], fcol(fi, c, 1), None,
                        ALU.mult)))
                    ch.op(lambda e: e.scalar_tensor_tensor(
                        dst[:, 0:Lo], src[:, 1:L:2], fcol(fi, c, 2), dst[:, 0:Lo],
                        ALU.mult, ALU.add))
                    ch.op(lambda e: e.scalar_tensor_tensor(
                        dst[:, 1:Lo], src[:, 1:L - 2:2], fcol(fi, c, 0),
                        dst[:, 1:Lo], ALU.mult, ALU.add))
                    ch.op(lambda e: e.scalar_tensor_tensor(
                        dst[:, 0:Lo - 1], src[:, 2:L:2], fcol(fi, c, 3),
                        dst[:, 0:Lo - 1], ALU.mult, ALU.add))
                    return ch.tok()

                def conv_s(dst, z, Lh, j, c, ch, first):
                    L = 2 * Lh
                    if first:
                        ch.op(lambda e: e.tensor_scalar(
                            dst[:, 0:L:2], z[:, 0:Lh], scol(j, c, 0), None,
                            ALU.mult))
                    else:
                        ch.op(lambda e: e.scalar_tensor_tensor(
                            dst[:, 0:L:2], z[:, 0:Lh], scol(j, c, 0),
                            dst[:, 0:L:2], ALU.mult, ALU.add))
                    ch.op(lambda e: e.scalar_tensor_tensor(
                        dst[:, 2:L:2], z[:, 0:Lh - 1], fcol(2 + j, c, 0),
                        dst[:, 2:L:2], ALU.mult, ALU.add))
                    ch.op(lambda e: e.scalar_tensor_tensor(
                        dst[:, 0:L - 2:2], z[:, 1:Lh], fcol(2 + j, c, 3),
                        dst[:, 0:L - 2:2], ALU.mult, ALU.add))
                    if first:
                        ch.op(lambda e: e.tensor_scalar(
                            dst[:, 1:L:2], z[:, 0:Lh], scol(j, c, 1), None,
                            ALU.mult))
                    else:
                        ch.op(lambda e: e.scalar_tensor_tensor(
                            dst[:, 1:L:2], z[:, 0:Lh], scol(j, c, 1),
                            dst[:, 1:L:2], ALU.mult, ALU.add))
                    ch.op(lambda e: e.scalar_tensor_tensor(
                        dst[:, 1:L - 2:2], z[:, 1:Lh], scol(j, c, 2),
                        dst[:, 1:L - 2:2], ALU.mult, ALU.add))
                    return ch.tok()

                tok_sdma = sq.tok()
                tok_r2fm_used = ve.tok()
                for c in range(NCH):
                    tld_c = sq.dma(r2fm[:], r2fm_dr[c * 128:(c + 1) * 128, :],
                                   deps=[tok_fmdma, tok_r2fm_used])
                    tprev = ve.tok()
                    for s in range(8):
                        pm1 = pe.op(lambda e, s=s: e.matmul(
                            ps_a[:, 0:512], onesr[0:1, :],
                            mr2_row[0:1, s * 512:(s + 1) * 512], start=True,
                            stop=True), deps=[tok_ev, tok_ev2, tprev], inc=True)
                        pm2 = pe.op(lambda e, s=s: e.matmul(
                            ps_b[:, 0:512], onesr[0:1, :],
                            rho2_row[0:1, s * 512:(s + 1) * 512], start=True,
                            stop=True), deps=[tprev], inc=True)
                        t1 = ve.op(lambda e, s=s: e.tensor_tensor(
                            sA[:, s * 512:(s + 1) * 512],
                            r2fm[:, s * 512:(s + 1) * 512],
                            ps_b[:, 0:512], ALU.mult),
                            deps=[pm2, tok_sdma, tld_c])
                        tprev = ve.op(lambda e, s=s: e.tensor_sub(
                            sA[:, s * 512:(s + 1) * 512],
                            sA[:, s * 512:(s + 1) * 512], ps_a[:, 0:512]),
                            deps=[pm1])
                    tk_x2 = ve.tok()
                    conv_a(l0b, sA, T, 0, c, ve)
                    tk_l0 = ve.tok()
                    conv_a(h0b, sA, T, 1, c, ve)
                    tk_h0 = ve.tok()
                    conv_a(lo2, l0b, 2048, 0, c, ve)
                    tk_lo2 = ve.tok()
                    conv_a(h1b, l0b, 2048, 1, c, ve)
                    tk_h1 = ve.tok()
                    t1 = ve.op(lambda e: e.tensor_scalar(qb[:], lo2[:], 0.0, None,
                                                         ALU.min), deps=[tk_lo2])
                    t2 = sc.op(lambda e: e.activation(qb[:], qb[:], AF.Exp),
                               deps=[t1])
                    t3 = ve.op(lambda e: e.tensor_scalar(num_sb[:], lo2[:], 0.0,
                                                         None, ALU.max))
                    t4 = ve.op(lambda e: e.tensor_add(qb[:], qb[:], num_sb[:]),
                               deps=[t2, t3])
                    t5 = ve.op(lambda e: e.tensor_reduce(st1[:], qb[:], axis=AX.X,
                                                         op=ALU.add), deps=[t4])
                    tok_tr_ev = ve.tok()
                    for tt in range(8):
                        p1 = pe.op(lambda e, tt=tt: e.transpose(
                            ps_a[:, 0:128], qb[:, tt * 128:(tt + 1) * 128],
                            identt[:]), deps=[t4, tok_tr_ev], inc=True)
                        p2 = pe.op(lambda e, tt=tt: e.transpose(
                            ps_a[:, 128:256], lo2[:, tt * 128:(tt + 1) * 128],
                            identt[:]), deps=[], inc=True)
                        e1 = ve.op(lambda e: e.tensor_copy(qT_sb[:], ps_a[:, 0:128]),
                                   deps=[p1])
                        e2 = ve.op(lambda e: e.tensor_copy(vT_sb[:],
                                                           ps_a[:, 128:256]),
                                   deps=[p2])
                        tok_tr_ev = e2
                        pe.op(lambda e, tt=tt: e.matmul(
                            ps_c[0:64, 0:64], qT_sb[:, 0:64], vT_sb[:, 0:64],
                            start=(tt == 0), stop=(tt == 7)),
                            deps=[e1, e2], inc=False)
                        pe.op(lambda e, tt=tt: e.matmul(
                            ps_d[64:128, 0:64], qT_sb[:, 64:128], vT_sb[:, 64:128],
                            start=(tt == 0), stop=(tt == 7)),
                            deps=[], inc=(tt == 7))
                    tkv = pe.tok()
                    ekv = ve.op(lambda e: e.tensor_copy(kv_sb[0:64, :],
                                                        ps_c[0:64, 0:64]),
                                deps=[tkv])
                    ekv = ve.op(lambda e: e.tensor_copy(kv_sb[64:128, :],
                                                        ps_d[64:128, 0:64]),
                                deps=[tkv])
                    pe.op(lambda e: e.matmul(ps_y[0:64, 0:512], kv_sb[0:64, 0:64],
                                             qb[0:64, 0:512], start=True, stop=True),
                          deps=[ekv], inc=False)
                    pe.op(lambda e: e.matmul(ps_y[0:64, 512:1024],
                                             kv_sb[0:64, 0:64], qb[0:64, 512:1024],
                                             start=True, stop=True), inc=False)
                    pe.op(lambda e: e.matmul(ps_y[64:128, 0:512],
                                             kv_sb[64:128, 0:64], qb[64:128, 0:512],
                                             start=True, stop=True), inc=False)
                    pe.op(lambda e: e.matmul(ps_y[64:128, 512:1024],
                                             kv_sb[64:128, 0:64],
                                             qb[64:128, 512:1024],
                                             start=True, stop=True), inc=True)
                    tnum = pe.tok()
                    pe.op(lambda e: e.matmul(ps_a[0:1, 0:512], st1[0:64, 0:1],
                                             qb[0:64, 0:512], start=True, stop=True),
                          deps=[t5], inc=False)
                    pe.op(lambda e: e.matmul(ps_b[0:1, 0:512], st1[0:64, 0:1],
                                             qb[0:64, 512:1024], start=True,
                                             stop=True), inc=False)
                    pe.op(lambda e: e.matmul(ps_c[0:1, 0:512], st1[64:128, 0:1],
                                             qb[64:128, 0:512], start=True,
                                             stop=True), inc=False)
                    pe.op(lambda e: e.matmul(ps_d[0:1, 0:512], st1[64:128, 0:1],
                                             qb[64:128, 512:1024], start=True,
                                             stop=True), inc=True)
                    tden = pe.tok()
                    d1 = ve.op(lambda e: e.tensor_copy(den_sb[0:1, 0:512],
                                                       ps_a[0:1, 0:512]),
                               deps=[tden])
                    ve.op(lambda e: e.tensor_copy(den_sb[0:1, 512:1024],
                                                  ps_b[0:1, 0:512]))
                    ve.op(lambda e: e.tensor_copy(den_sb[0:1, 1024:1536],
                                                  ps_c[0:1, 0:512]))
                    ve.op(lambda e: e.tensor_copy(den_sb[0:1, 1536:2048],
                                                  ps_d[0:1, 0:512]))
                    ve.op(lambda e: e.tensor_scalar(den_sb[:], den_sb[:],
                                                    ATTN_EPS, None, ALU.add))
                    trc = ve.op(lambda e: e.reciprocal(den_sb[:], den_sb[:]))
                    pe.op(lambda e: e.matmul(ps_w[0:64, 0:512], onesr[0:1, 0:64],
                                             den_sb[0:1, 0:512], start=True,
                                             stop=True), deps=[trc], inc=False)
                    pe.op(lambda e: e.matmul(ps_w[0:64, 512:1024], onesr[0:1, 0:64],
                                             den_sb[0:1, 512:1024], start=True,
                                             stop=True), inc=False)
                    pe.op(lambda e: e.matmul(ps_w[64:128, 0:512], onesr[0:1, 0:64],
                                             den_sb[0:1, 1024:1536], start=True,
                                             stop=True), inc=False)
                    pe.op(lambda e: e.matmul(ps_w[64:128, 512:1024],
                                             onesr[0:1, 0:64],
                                             den_sb[0:1, 1536:2048], start=True,
                                             stop=True), inc=True)
                    tbc = pe.tok()
                    en = ve.op(lambda e: e.tensor_copy(num_sb[:], ps_y[:, :]),
                               deps=[tnum])
                    tat = ve.op(lambda e: e.tensor_tensor(qb[:], num_sb[:],
                                                          ps_w[:, :], ALU.mult),
                                deps=[en, tbc])
                    ve.wait(tk_h1)
                    conv_s(l0b, qb, 1024, 0, c, ve, first=True)
                    conv_s(l0b, h1b, 1024, 1, c, ve, first=False)
                    conv_s(sA, l0b, 2048, 0, c, ve, first=False)
                    ve.wait(tk_h0)
                    conv_s(sA, h0b, 2048, 1, c, ve, first=False)
                    tk_sc = ve.tok()
                    tok_r2fm_used = tk_sc
                    tok_sdma = sq.dma(s_dr[c * 128:(c + 1) * 128, :], sA[:],
                                      deps=[tk_sc])
                k.barrier()

        if stage >= 2:
            with ExitStack() as c2:
                fs2 = make_ffn_scope(c2, "c2")
                gam_b = sb("p_gamb", [128, H], ctx=c2)
                bet_b = sb("p_betb", [128, H], ctx=c2)
                tgr = sq.dma(gam_b[0:1, :], gam_in[:, :])
                tbr = sq.dma(bet_b[0:1, :], bet_in[:, :])
                s_sb = fs2["fmsc"]
                stok_t = sb("p_stok", [128, H], ctx=c2)
                at_t = fs2["r_t"]
                r2t_t = fs2["yf_t"]
                mu4 = sb("p_mu4", [128, 1], ctx=c2)
                rho4 = sb("p_rho4", [128, 1], ctx=c2)

                for bc in (gam_b, bet_b):
                    for s in range(2):
                        pm = pe.op(lambda e, bc=bc, s=s: e.matmul(
                            ps_a[:, 0:512], onesr[0:1, :],
                            bc[0:1, s * 512:(s + 1) * 512], start=True, stop=True),
                            deps=[ve.tok(), tgr, tbr], inc=True)
                        ve.op(lambda e, bc=bc, s=s: e.tensor_copy(
                            bc[:, s * 512:(s + 1) * 512], ps_a[:, 0:512]),
                            deps=[pm])
                tk_gb = ve.tok()

                tok_r3dma = sq.tok()

                def make_tile2(i):
                    nonlocal tok_r3dma
                    tl2 = sq.dma(r2t_t[:], r2_dr[i * 128:(i + 1) * 128, :],
                                 deps=[ve.tok()])
                    tl = sq.dma(s_sb[:, :].rearrange("p (c t) -> p c t", c=NCH),
                                s_dr[:, i * 128:(i + 1) * 128]
                                .rearrange("(c p) t -> p c t", p=128),
                                deps=[ve.tok(), sc.tok(), pe.tok()])
                    for c in range(NCH):
                        pe.op(lambda e, c=c: e.transpose(
                            ps_y[:, c * 128:(c + 1) * 128],
                            s_sb[:, c * 128:(c + 1) * 128], identt[:]),
                            deps=([tl, ve.tok()] if c == 0 else ()),
                            inc=(c == NCH - 1))
                    ptr = pe.tok()
                    tse = ve.op(lambda e: e.tensor_copy(stok_t[:], ps_y[:, :]),
                                deps=[ptr])
                    t1 = ve.op(lambda e: e.tensor_reduce(st1[:], stok_t[:],
                                                         axis=AX.X, op=ALU.add))
                    t2 = sc.op(lambda e: e.activation(fs2["sq_t"][:], stok_t[:],
                                                      AF.Square,
                                                      accum_out=st2[:, 0:1]),
                               deps=[tse])
                    ve.op(lambda e: e.tensor_scalar(mu4[:], st1[:], 1.0 / H, None,
                                                    ALU.mult), deps=[t1])
                    ve.op(lambda e: e.tensor_tensor(st3[:], mu4[:], mu4[:],
                                                    ALU.mult))
                    ve.op(lambda e: e.scalar_tensor_tensor(
                        st3[:], st2[:], 1.0 / H, st3[:], ALU.mult, ALU.subtract),
                        deps=[t2])
                    ve.op(lambda e: e.tensor_scalar(st3[:], st3[:], LN_EPS, None,
                                                    ALU.add))
                    ve.op(lambda e: e.reciprocal(st4[:], st3[:]))
                    tr4 = sc.op(lambda e: e.activation(rho4[:], st4[:], AF.Sqrt),
                                deps=[ve.tok()])
                    ta1 = ve.op(lambda e: e.tensor_scalar(
                        at_t[:], stok_t[:], mu4[:, 0:1], rho4[:, 0:1],
                        ALU.subtract, ALU.mult), deps=[tr4])
                    ta2 = ve.op(lambda e: e.tensor_mul(at_t[:], at_t[:], gam_b[:]),
                                deps=[tk_gb])
                    ta3 = ve.op(lambda e: e.tensor_add(at_t[:], at_t[:], bet_b[:]))
                    tr3 = ve.op(lambda e: e.tensor_add(fs2["xt_f"][:], r2t_t[:],
                                                       at_t[:]),
                                deps=[ta3, tl2, sc.tok(), pe.tok()])
                    tok_r3dma = sq.dma(r3_dr[i * 128:(i + 1) * 128, :],
                                       fs2["xt_f"][:], deps=[tr3])
                    if stage == 2:
                        tok_r3dma = sq.dma(out_dr[i * 128:(i + 1) * 128, :],
                                           fs2["xt_f"][:], deps=[tr3])
                    return tr3

                if stage == 2:
                    for i in range(NT):
                        make_tile2(i)
                    k.barrier()
                else:
                    tok_out2 = sq.tok()

                    def out_write2(i, txf, tys):
                        nonlocal tok_out2
                        tl = sq.dma(fs2["r_t"][:],
                                    r3_dr[i * 128:(i + 1) * 128, :],
                                    deps=[ve.tok(), tok_r3dma])
                        t1 = ve.op(lambda e: e.scalar_tensor_tensor(
                            fs2["r_t"][:], fs2["xbf_t"][:], 0.5, fs2["r_t"][:],
                            ALU.mult, ALU.add), deps=[txf, tl, tok_out2])
                        t2 = ve.op(lambda e: e.tensor_add(
                            fs2["r_t"][:], fs2["r_t"][:], fs2["yf_t"][:]),
                            deps=[tys])
                        tok_out2 = sq.dma(out_dr[i * 128:(i + 1) * 128, :],
                                          fs2["r_t"][:], deps=[t2])
                        return tok_out2

                    emit_ffn(2, fs2, make_tile2, out_write2)
                    k.barrier()

        k.barrier()

        with nc.Block() as block:
            k.emit(block)

    return nc


def kernel(L0_attn_x, L0_attn_h_filter, L0_attn_g_filter, L0_attn_h_synth,
           L0_attn_g_synth, L0_attn_norm_gamma, L0_attn_norm_beta,
           L0_ffn_w_router, L0_ffn_w_gate, L0_ffn_w_up, L0_ffn_w_down):
    os.environ["TRN_ALLOW_ICI_ALL_TO_ALL"] = "1"
    x = np.ascontiguousarray(L0_attn_x, dtype=np.float32)
    nc = build()
    in_maps = []
    for r in range(8):
        mprev = np.zeros((8, 1), np.float32)
        mprev[:r] = 1.0
        osel = np.zeros((NE, EPC), np.float32)
        for j in range(EPC):
            osel[4 * r + j, j] = 1.0
        in_maps.append({
            "x": np.ascontiguousarray(x[r]),
            "w_router": np.ascontiguousarray(L0_ffn_w_router, np.float32),
            "h_filter": np.ascontiguousarray(L0_attn_h_filter, np.float32),
            "g_filter": np.ascontiguousarray(L0_attn_g_filter, np.float32),
            "h_synth": np.ascontiguousarray(L0_attn_h_synth, np.float32),
            "g_synth": np.ascontiguousarray(L0_attn_g_synth, np.float32),
            "gamma": np.ascontiguousarray(L0_attn_norm_gamma, np.float32).reshape(1, H),
            "beta": np.ascontiguousarray(L0_attn_norm_beta, np.float32).reshape(1, H),
            "wg": np.ascontiguousarray(L0_ffn_w_gate[4 * r:4 * r + 4], np.float32),
            "wu": np.ascontiguousarray(L0_ffn_w_up[4 * r:4 * r + 4], np.float32),
            "wd": np.ascontiguousarray(L0_ffn_w_down[4 * r:4 * r + 4], np.float32),
            "mask_prev": mprev,
            "own_sel": osel,
        })
    res = run_bass_kernel_spmd(nc, in_maps, core_ids=list(range(8)))
    out = np.stack([res.results[r]["out"] for r in range(8)], axis=0)
    return out


# version 02 (final)
# speedup vs baseline: 1.0529x; 1.0529x over previous
import os
import numpy as np
import concourse.bass as bass
import concourse.mybir as mybir
from concourse.bass_utils import run_bass_kernel_spmd
from concourse.bass_core_rust import IndirectOffsetOnAxis
from contextlib import ExitStack

F32 = mybir.dt.float32
BF16 = mybir.dt.bfloat16
U32 = mybir.dt.uint32
AF = mybir.ActivationFunctionType
ALU = mybir.AluOpType
AX = mybir.AxisListType

STAGE = 3
CUT = 999

T = 4096
H = 1024
NCH = 8
NT = 32
NE = 32
EPC = 4
CAP = 1280
PCAP = 768
AR = 8 * PCAP
ZROW = float(AR)
NG = CAP // 128
LN_EPS = 1e-5
ATTN_EPS = 1e-6
RG = [[0, 1, 2, 3, 4, 5, 6, 7]]
BLK = [(0, 512), (512, 512), (1024, 256)]


class _CCTok:
    def __init__(self, sem):
        self.sem = sem


class Chain:
    def __init__(self, k, ename, self_sync=True):
        self.k = k
        self.ename = ename
        self.sem = k.ctx.enter_context(k.nc.semaphore(f"ch_{ename}"))
        self.cum = 0
        self.self_sync = self_sync
        self.hw = {}

    def tok(self):
        return (self, self.cum)

    def _emit(self, f):
        import traceback
        site = traceback.extract_stack()[-4:-2]
        def wrapped(e, f=f, site=site):
            try:
                return f(e)
            except Exception as ex:
                raise RuntimeError(f"emit failed; scheduled at {site}") from ex
        self.k.prog[self.ename].append(wrapped)

    def wait(self, tok):
        ch, val = tok
        if val <= 0:
            return
        if ch is self:
            if self.self_sync:
                return
            if self.hw.get(self.sem.num, -1) >= val:
                return
            sem = self.sem
            self._emit(lambda e: e.wait_ge(sem, val))
            self.hw[self.sem.num] = val
            return
        if self.hw.get(ch.sem.num, -1) >= val:
            return
        sem = ch.sem
        self._emit(lambda e: e.wait_ge(sem, val))
        self.hw[ch.sem.num] = val

    def _self_wait(self):
        if self.self_sync and self.cum > 0:
            if self.hw.get(self.sem.num, -1) < self.cum:
                sem, val = self.sem, self.cum
                self._emit(lambda e: e.wait_ge(sem, val))
                self.hw[self.sem.num] = val

    def op(self, f, deps=(), inc=True, step=1):
        for d in deps:
            self.wait(d)
        self._self_wait()
        if inc:
            sem, s = self.sem, step
            self._emit(lambda e: f(e).then_inc(sem, s))
            self.cum += s
        else:
            self._emit(lambda e: f(e))
        return self.tok()

    def dma(self, out, in_, deps=(), **kw):
        try:
            od = out.tensor.space if hasattr(out, "tensor") else None
            idd = in_.tensor.space if hasattr(in_, "tensor") else None
        except Exception:
            od = idd = None
        return self.op(lambda e: e.dma_start(out=out, in_=in_, **kw),
                       deps=deps, inc=True, step=16)

    def dmat(self, out, in_, deps=()):
        return self.op(lambda e: e.dma_start_transpose(out, in_),
                       deps=deps, inc=True, step=16)

    def igather(self, out, in_, idx, deps=(), **kw):
        return self.op(lambda e: e.indirect_dma_start(
            out=out, out_offset=None, in_=in_,
            in_offset=IndirectOffsetOnAxis(idx, 0), **kw),
            deps=deps, inc=True, step=16)

    def iscatter(self, out, idx, in_, deps=(), **kw):
        return self.op(lambda e: e.indirect_dma_start(
            out=out, out_offset=IndirectOffsetOnAxis(idx, 0),
            in_=in_, in_offset=None, **kw),
            deps=deps, inc=True, step=16)


class K:
    def __init__(self, nc, ctx):
        self.nc = nc
        self.ctx = ctx
        self.prog = {n: [] for n in ("sync", "vector", "scalar", "tensor", "gpsimd")}
        self.ve = Chain(self, "vector")
        self.sc = Chain(self, "scalar")
        self.pe = Chain(self, "tensor", self_sync=False)
        self.gp = Chain(self, "gpsimd")
        self.sq = Chain(self, "sync", self_sync=False)
        self.chains = [self.ve, self.sc, self.pe, self.gp, self.sq]
        self._ncc = 0

    def collective(self, kind, ins, outs, deps=()):
        sem = self.ctx.enter_context(self.nc.semaphore(f"cc{self._ncc}"))
        self._ncc += 1
        ch = self.gp
        for d in deps:
            ch.wait(d)
        ch._self_wait()
        ch._emit(lambda e: e.collective_compute(
            kind, ALU.bypass, replica_groups=RG, ins=ins, outs=outs
        ).then_inc(sem, 1))
        return (_CCTok(sem), 1)

    def barrier(self):
        for a in self.chains:
            for b in self.chains:
                a.wait(b.tok())

    def emit(self, block):
        block.sync(lambda e: [f(e) for f in self.prog["sync"]])
        block.vector(lambda e: [f(e) for f in self.prog["vector"]])
        block.scalar(lambda e: [f(e) for f in self.prog["scalar"]])
        block.tensor(lambda e: [f(e) for f in self.prog["tensor"]])
        block.gpsimd(lambda e: [f(e) for f in self.prog["gpsimd"]])


def _consts(nc):
    c = {}
    c["ident"] = nc.inline_tensor(np.eye(128, dtype=np.float32), "c_ident").ap()
    import ml_dtypes as _mld
    c["identb"] = nc.inline_tensor(np.eye(128, dtype=_mld.bfloat16), "c_identb").ap()
    c["iotaB"] = nc.inline_tensor(
        np.tile(np.arange(NE, dtype=np.float32) + 1e6, (128, 1)), "c_iotaB").ap()
    c["iota32col"] = nc.inline_tensor(
        np.arange(NE, dtype=np.float32).reshape(NE, 1), "c_iota32col").ap()
    c["ones_1x128"] = nc.inline_tensor(np.ones((1, 128), np.float32), "c_o1x128").ap()
    c["ones_1x32"] = nc.inline_tensor(np.ones((1, 32), np.float32), "c_o1x32").ap()
    c["ones_32x1"] = nc.inline_tensor(np.ones((32, 1), np.float32), "c_o32x1").ap()
    c["ones_8x1"] = nc.inline_tensor(np.ones((8, 1), np.float32), "c_o8x1").ap()
    c["ones_128x1"] = nc.inline_tensor(np.ones((128, 1), np.float32), "c_o128x1").ap()
    c["iotaJ"] = nc.inline_tensor(
        np.tile(np.arange(CAP, dtype=np.float32), (NE, 1)), "c_iotaJ").ap()
    bsel = np.zeros((NE, EPC), np.float32)
    for p in range(NE):
        bsel[p, p // 8] = 1.0
    c["blocksel"] = nc.inline_tensor(bsel, "c_blocksel").ap()
    c["sbase48"] = nc.inline_tensor(
        np.tile(np.arange(8, dtype=np.float32) * PCAP, (4, 1)), "c_sbase48").ap()
    c["dstb84m1"] = nc.inline_tensor(
        np.tile((np.arange(8, dtype=np.float32) * PCAP).reshape(8, 1), (1, 4)) - 1.0,
        "c_dstb84m1").ap()
    return c


def build(stage=STAGE):
    os.environ["TRN_ALLOW_ICI_ALL_TO_ALL"] = "1"
    nc = bass.Bass("TRN2", target_bir_lowering=False)

    x_in = nc.dram_tensor("x", [T, H], F32, kind="ExternalInput").ap()
    wr_in = nc.dram_tensor("w_router", [H, NE], F32, kind="ExternalInput").ap()
    hf_in = nc.dram_tensor("h_filter", [4, H], F32, kind="ExternalInput").ap()
    gf_in = nc.dram_tensor("g_filter", [4, H], F32, kind="ExternalInput").ap()
    hs_in = nc.dram_tensor("h_synth", [4, H], F32, kind="ExternalInput").ap()
    gs_in = nc.dram_tensor("g_synth", [4, H], F32, kind="ExternalInput").ap()
    gam_in = nc.dram_tensor("gamma", [1, H], F32, kind="ExternalInput").ap()
    bet_in = nc.dram_tensor("beta", [1, H], F32, kind="ExternalInput").ap()
    wg_in = nc.dram_tensor("wg", [EPC, H, H], F32, kind="ExternalInput").ap()
    wu_in = nc.dram_tensor("wu", [EPC, H, H], F32, kind="ExternalInput").ap()
    wd_in = nc.dram_tensor("wd", [EPC, H, H], F32, kind="ExternalInput").ap()
    mprev_in = nc.dram_tensor("mask_prev", [8, 1], F32, kind="ExternalInput").ap()
    osel_in = nc.dram_tensor("own_sel", [NE, EPC], F32, kind="ExternalInput").ap()
    out_dr = nc.dram_tensor("out", [T, H], F32, kind="ExternalOutput").ap()
    DBG = False
    if DBG:
        dbg_strip = nc.dram_tensor("dbg_strip", [NE, 1024 + 32], F32,
                                   kind="ExternalOutput").ap()
        dbg_cols = nc.dram_tensor("dbg_cols", [128, 5 * NT], F32,
                                  kind="ExternalOutput").ap()
        dbg_idx = nc.dram_tensor("dbg_idx", [128, NG * EPC], U32,
                                 kind="ExternalOutput").ap()
        dbg_cnt = nc.dram_tensor("dbg_cnt", [8, NE], F32,
                                 kind="ExternalOutput").ap()
        dbg_sm = nc.dram_tensor("dbg_sm", [NE, 24], F32,
                                kind="ExternalOutput").ap()

    stash = {
        "g": nc.dram_tensor("stash_g", [EPC, H, H], BF16, kind="Internal").ap(),
        "u": nc.dram_tensor("stash_u", [EPC, H, H], BF16, kind="Internal").ap(),
        "d": nc.dram_tensor("stash_d", [EPC, H, H], BF16, kind="Internal").ap(),
    }
    xln_dr = nc.dram_tensor("xln_dr", [T, H], BF16, kind="Internal").ap()
    send_dr = nc.dram_tensor("send_dr", [AR, H], BF16, kind="Internal").ap()
    recv_dr = nc.dram_tensor("recv_dr", [AR + 1, H], BF16, kind="Internal").ap()
    rsrc_dr = nc.dram_tensor("rsrc_dr", [AR + 1, H], BF16, kind="Internal").ap()
    rbuf_dr = nc.dram_tensor("rbuf_dr", [AR + 1, H], BF16, kind="Internal").ap()
    gath_dr = nc.dram_tensor("gath_dr", [2, CAP, H], BF16, kind="Internal").ap()
    r2_dr = nc.dram_tensor("r2_dr", [T, H], F32, kind="Internal").ap()
    r3_dr = nc.dram_tensor("r3_dr", [T, H], F32, kind="Internal").ap()
    s_dr = nc.dram_tensor("s_dr", [H, T], F32, kind="Internal").ap()
    cnt_src = nc.dram_tensor("cnt_src", [1, NE], F32, kind="Internal").ap()
    cnt_dst = nc.dram_tensor("cnt_dst", [8, NE], F32, kind="Internal").ap()
    b32a = nc.dram_tensor("b32a", [NE], F32, kind="Internal").ap()
    b32b = nc.dram_tensor("b32b", [NE], F32, kind="Internal").ap()
    b32c = nc.dram_tensor("b32c", [NE], F32, kind="Internal").ap()

    C = _consts(nc)

    with ExitStack() as top:
        k = K(nc, top)
        ve, sc, pe, gp, sq = k.ve, k.sc, k.pe, k.gp, k.sq

        def sb(name, shape, dt=F32, ctx=top):
            return ctx.enter_context(nc.sbuf_tensor(name, shape, dt))

        ps_a = top.enter_context(nc.psum_tensor("ps_a", [128, 512], F32))
        ps_b = top.enter_context(nc.psum_tensor("ps_b", [128, 512], F32))
        ps_c = top.enter_context(nc.psum_tensor("ps_c", [128, 512], F32))
        ps_xeb = top.enter_context(nc.psum_tensor("ps_xeb", [128, 1024], BF16))
        ps_y = top.enter_context(nc.psum_tensor("ps_y", [128, 1024], F32))
        ps_w = top.enter_context(nc.psum_tensor("ps_w", [128, 1024], F32))

        identt = sb("identt", [128, 128])
        identb = sb("identb", [128, 128], BF16)
        wr_sb = sb("wr_sb", [128, NCH * NE])
        csum_b = sb("csum_b", [128, NE])
        fil = sb("fil", [128, NCH * 4 * 4])
        fsyn = sb("fsyn", [128, NCH * 3 * 2])
        onesr = sb("onesr", [1, 128])
        ones32r = sb("ones32r", [1, 32])
        ones32c = sb("ones32c", [NE, 1])
        ones8c = sb("ones8c", [8, 1])
        ones128c = sb("ones128c", [128, 1])
        iotaBt = sb("iotaBt", [128, NE])
        iota32c = sb("iota32c", [NE, 1])
        mprev = sb("mprev", [8, 1])
        osel = sb("osel", [NE, EPC])
        iotaJt = sb("iotaJt", [NE, CAP])
        bselt = sb("bselt", [NE, EPC])
        sbase48 = sb("sbase48", [4, 8])
        dstb84m1 = sb("dstb84m1", [8, 4])
        mu2_c = sb("mu2_c", [128, NT])
        rho2_c = sb("rho2_c", [128, NT])
        st1 = sb("st1", [128, 1])
        st2 = sb("st2", [128, 1])
        st3 = sb("st3", [128, 1])
        st4 = sb("st4", [128, 1])
        csrow = sb("csrow", [1, NE])

        bc_reg = nc.gpsimd.alloc_register("bc_reg")
        gp.op(lambda e: e.reg_mov(bc_reg, AR - 1), inc=False)
        k.bc_reg = bc_reg

        tk_id = sq.dma(identt[:], C["ident"][:, :])
        sq.dma(identb[:], C["identb"][:, :])
        sq.dma(onesr[:], C["ones_1x128"][:, :])
        sq.dma(ones32r[:], C["ones_1x32"][:, :])
        sq.dma(ones32c[:], C["ones_32x1"][:, :])
        sq.dma(ones8c[:], C["ones_8x1"][:, :])
        sq.dma(ones128c[:], C["ones_128x1"][:, :])
        sq.dma(iotaBt[:], C["iotaB"][:, :])
        sq.dma(iota32c[:], C["iota32col"][:, :])
        sq.dma(mprev[:], mprev_in[:, :])
        sq.dma(osel[:], osel_in[:, :])
        sq.dma(iotaJt[:], C["iotaJ"][:, :])
        sq.dma(bselt[:], C["blocksel"][:, :])
        sq.dma(sbase48[:], C["sbase48"][:, :])
        sq.dma(dstb84m1[:], C["dstb84m1"][:, :])
        for c in range(NCH):
            sq.dma(wr_sb[:, c * NE:(c + 1) * NE], wr_in[c * 128:(c + 1) * 128, :])
        tk_t0 = sq.tok()

        t0ctx = ExitStack()
        wcast = sb("wcast", [128, 2048], BF16, ctx=t0ctx)
        stash_tok = {}
        tok_wc_st = sq.tok()
        for e in range(EPC):
            for nm, wsrc in (("g", wg_in), ("u", wu_in), ("d", wd_in)):
                for half in range(4):
                    tc = gp.dma(wcast[:, :].rearrange("p (a f) -> p a f", a=2),
                                wsrc[e].rearrange("(a p) f -> p a f", p=128)
                                [:, half * 2:(half + 1) * 2, :],
                                deps=[tok_wc_st])
                    tok_wc_st = sq.dma(
                        stash[nm][e].rearrange("(a p) f -> p a f", p=128)
                        [:, half * 2:(half + 1) * 2, :],
                        wcast[:, :].rearrange("p (a f) -> p a f", a=2),
                        deps=[tc])
                stash_tok[(nm, e)] = tok_wc_st

        zb = sb("zb", [128, H], BF16, ctx=t0ctx)
        tk_zb = ve.op(lambda e: e.memset(zb[:], 0.0))
        for buf, tot in ((send_dr, AR), (recv_dr, AR + 1), (rsrc_dr, AR + 1),
                         (rbuf_dr, AR + 1)):
            r0 = 0
            while r0 < tot:
                n = min(128, tot - r0)
                sq.dma(buf[r0:r0 + n, :], zb[0:n, :], deps=[tk_zb])
                r0 += n
        tk_zbufs = sq.tok()

        filt_rows = sb("filt_rows", [4, H], ctx=t0ctx)
        for fi, src in enumerate((hf_in, gf_in, hs_in, gs_in)):
            tkl = sq.dma(filt_rows[:], src[:, :], deps=[ve.tok()])
            for c in range(NCH):
                pt = pe.op(lambda e, c=c: e.transpose(
                    ps_a[:, 0:4], filt_rows[0:4, c * 128:(c + 1) * 128],
                    identt[0:4, 0:4]), deps=[tkl, tk_id, ve.tok()], inc=True)
                ve.op(lambda e, fi=fi, c=c: e.tensor_copy(
                    fil[:, (fi * NCH + c) * 4:(fi * NCH + c) * 4 + 4], ps_a[:, 0:4]),
                    deps=[pt])
        for j, fi in enumerate((2, 3)):
            for c in range(NCH):
                b = (fi * NCH + c) * 4
                o = (j * NCH + c) * 3
                ve.op(lambda e, b=b, o=o: e.tensor_add(
                    fsyn[:, o:o + 1], fil[:, b + 1:b + 2], fil[:, b + 2:b + 3]))
                ve.op(lambda e, b=b, o=o: e.tensor_add(
                    fsyn[:, o + 1:o + 2], fil[:, b:b + 1], fil[:, b + 1:b + 2]))
                ve.op(lambda e, b=b, o=o: e.tensor_add(
                    fsyn[:, o + 2:o + 3], fil[:, b + 2:b + 3], fil[:, b + 3:b + 4]))
        tk_fil = ve.tok()

        for c in range(NCH):
            pe.op(lambda e, c=c: e.matmul(
                ps_b[0:1, 0:NE], ones128c[:, 0:1], wr_sb[:, c * NE:(c + 1) * NE],
                start=(c == 0), stop=(c == NCH - 1)),
                deps=([tk_t0] if c == 0 else ()), inc=(c == NCH - 1))
        t = ve.op(lambda e: e.tensor_copy(csrow[:], ps_b[0:1, 0:NE]), deps=[pe.tok()])
        pt = pe.op(lambda e: e.matmul(ps_a[:, 0:NE], onesr[0:1, :], csrow[0:1, :],
                                      start=True, stop=True), deps=[t], inc=True)
        ve.op(lambda e: e.tensor_copy(csum_b[:], ps_a[:, 0:NE]), deps=[pt])
        tk_setup = ve.tok()
        k.barrier()
        t0ctx.close()

        def emit_ffn(call, fs, make_tile, out_write):
            rctx = ExitStack()
            fs["ohT"] = rctx.enter_context(nc.sbuf_tensor(f"r{call}_ohT", [NE, T], F32))
            fs["crun"] = rctx.enter_context(nc.sbuf_tensor(f"r{call}_crun", [NE, T], F32))
            fs["comb_s"] = rctx.enter_context(nc.sbuf_tensor(f"r{call}_cmbs", [NE, 512], F32))
            fs["eidx_row"] = rctx.enter_context(nc.sbuf_tensor(f"r{call}_er", [1, 512], F32))
            fs["rowscr"] = rctx.enter_context(nc.sbuf_tensor(f"r{call}_rsc", [1, 512], F32))
            zcol = fs["zcol"]
            xt_f = fs["xt_f"]; xb_t = fs["xb_t"]; sq_t = fs["sq_t"]
            fmsc = fs["fmsc"]; logit_t = fs["logit_t"]; oh_t = fs["oh_t"]
            msk_t = fs["msk_t"]; nrmax = fs["nrmax"]; sume = fs["sume"]
            gcol = fs["gcol"]; ecol = fs["ecol"]; mu_c = fs["mu_c"]
            rho_c = fs["rho_c"]; eidx_row = fs["eidx_row"]
            ohT = fs["ohT"]; crun = fs["crun"]; comb_s = fs["comb_s"]
            fs["idx4"] = fs["crun"][0:4, 0:CAP]
            rowscr = fs["rowscr"]; slot_c = fs["slot_c"]; posg_c = fs["posg_c"]
            gate_c = fs["gate_c"]; acc_c = fs["acc_c"]; gw_c = fs["gw_c"]
            slot_u = fs["slot_u"]; gret_u = fs["gret_u"]
            gfin_c = fs["gfin_c"]; idxcol = fs["idxcol"]
            cnt84 = fs["cnt84"]; m48a = fs["m48a"]
            m48b = fs["m48b"]; m48c = fs["m48c"]; m84a = fs["m84a"]; m84b = fs["m84b"]
            cnteT = fs["cnteT"]; cntd_sb = fs["cntd_sb"]; thr_col = fs["thr_col"]
            dlt_col = fs["dlt_col"]; idx4 = fs["idx4"]; cntrow = fs["cntrow"]
            basem1 = fs["basem1"]
            gbuf2 = fs["gbuf2"]
            xe_sb = fs["xe_sb"]; wg_sb = fs["wg_sb"]; wu_sb = fs["wu_sb"]
            wd_sb = fs["wd_sb"]; a_sb = fs["a_sb"]; sg_sc = fs["sg_sc"]
            y_sb = fs["y_sb"]; yb_t = fs["yb_t"]; yf_t = fs["yf_t"]
            xbf_t = fs["xbf_t"]; r_t = fs["r_t"]

            tk_zc = ve.op(lambda e: e.memset(zcol[:], 0.0))

            tok_xb_dma = sq.tok()
            tok_fm_evac = ve.tok()
            tok_router_read = ve.tok()
            for i in range(NT):
                ttile = make_tile(i)
                t1 = ve.op(lambda e: e.tensor_reduce(st1[:], xt_f[:], axis=AX.X,
                                                     op=ALU.add), deps=[ttile])
                t2 = sc.op(lambda e: e.activation(sq_t[:], xt_f[:], AF.Square,
                                                  accum_out=st2[:, 0:1]), deps=[ttile])
                ve.op(lambda e, i=i: e.tensor_scalar(mu_c[:, i:i + 1], st1[:],
                                                     1.0 / H, None, ALU.mult),
                      deps=[t1])
                ve.op(lambda e, i=i: e.tensor_tensor(st3[:], mu_c[:, i:i + 1],
                                                     mu_c[:, i:i + 1], ALU.mult))
                ve.op(lambda e: e.scalar_tensor_tensor(st3[:], st2[:], 1.0 / H,
                                                       st3[:], ALU.mult,
                                                       ALU.subtract), deps=[t2])
                ve.op(lambda e: e.tensor_scalar(st3[:], st3[:], LN_EPS, None, ALU.add))
                ve.op(lambda e: e.reciprocal(st4[:], st3[:]))
                trho = sc.op(lambda e, i=i: e.activation(rho_c[:, i:i + 1], st4[:],
                                                         AF.Sqrt), deps=[ve.tok()])
                tstg = ve.op(lambda e, i=i: e.tensor_scalar(
                    xb_t[:], xt_f[:], mu_c[:, i:i + 1], rho_c[:, i:i + 1],
                    ALU.subtract, ALU.mult), deps=[trho, tok_xb_dma])
                tok_xb_dma = sq.dma(xln_dr[i * 128:(i + 1) * 128, :], xb_t[:],
                                    deps=[tstg])
                for c in range(NCH):
                    pe.op(lambda e, c=c: e.transpose(
                        ps_y[:, c * 128:(c + 1) * 128],
                        xt_f[:, c * 128:(c + 1) * 128], identt[:]),
                        deps=([ttile, tok_fm_evac, tk_id] if c == 0 else ()),
                        inc=(c == NCH - 1))
                ttr = pe.tok()
                tok_fm_evac = ve.op(lambda e: e.tensor_copy(fmsc[:], ps_y[:, :]),
                                    deps=[ttr])
                for c in range(NCH):
                    pe.op(lambda e, c=c: e.matmul(
                        ps_c[:, 0:NE], fmsc[:, c * 128:(c + 1) * 128],
                        wr_sb[:, c * NE:(c + 1) * NE],
                        start=(c == 0), stop=(c == NCH - 1)),
                        deps=([tok_fm_evac, tok_router_read, tk_setup]
                              if c == 0 else ()),
                        inc=(c == NCH - 1))
                traw = pe.tok()
                t3 = ve.op(lambda e, i=i: e.scalar_tensor_tensor(
                    logit_t[:], csum_b[:], mu_c[:, i:i + 1], ps_c[:, 0:NE],
                    ALU.mult, ALU.subtract), deps=[traw])
                tok_router_read = t3
                t4 = ve.op(lambda e, i=i: e.tensor_scalar(st3[:], rho_c[:, i:i + 1],
                                                          -1.0, None, ALU.mult))
                t5 = ve.op(lambda e: e.tensor_scalar(logit_t[:], logit_t[:],
                                                     st3[:, 0:1], None, ALU.mult),
                           deps=[t3, t4])
                t6 = ve.op(lambda e: e.reduce_max(nrmax[:], logit_t[:], axis=AX.X,
                                                  negate=True), deps=[t5])
                t7 = sc.op(lambda e: e.activation(msk_t[:], logit_t[:], AF.Exp,
                                                  bias=nrmax[:, 0:1], scale=1.0,
                                                  accum_out=sume[:, 0:1]), deps=[t6])
                ve.op(lambda e: e.reciprocal(gcol[:], sume[:]), deps=[t7])
                t8 = ve.op(lambda e: e.tensor_scalar(oh_t[:], logit_t[:],
                                                     nrmax[:, 0:1], 0.0,
                                                     ALU.add, ALU.is_ge), deps=[t6])
                t9 = ve.op(lambda e: e.scalar_tensor_tensor(
                    msk_t[:], oh_t[:], -1e6, iotaBt[:], ALU.mult, ALU.add), deps=[t8])
                tec = ve.op(lambda e: e.tensor_reduce(ecol[:], msk_t[:], axis=AX.X,
                                                      op=ALU.min), deps=[t9])
                ve.op(lambda e, i=i: e.tensor_copy(gate_c[:, i:i + 1], gcol[:]))
                pt1 = pe.op(lambda e: e.transpose(ps_a[0:1, 0:128], ecol[:, 0:1],
                                                  identt[:]), deps=[tec], inc=True)
                tev = ve.op(lambda e, i=i: e.tensor_copy(
                    eidx_row[0:1, (i % 4) * 128:(i % 4 + 1) * 128],
                    ps_a[0:1, 0:128]), deps=[pt1])
                if i % 4 == 3:
                    s = i // 4
                    pm = pe.op(lambda e: e.matmul(
                        ps_b[0:32, 0:512], ones32r[0:1, :], eidx_row[0:1, :],
                        start=True, stop=True), deps=[tev], inc=True)
                    ve.op(lambda e, s=s: e.tensor_scalar(
                        ohT[:, s * 512:(s + 1) * 512], ps_b[0:32, 0:512],
                        iota32c[:, 0:1], None, ALU.is_equal), deps=[pm])

            tk_rows = ve.tok()
            if call == 2 and CUT <= 1:
                return

            tprev = tk_rows
            tscan = ve.op(lambda e: e.tensor_tensor_scan(
                crun[:, :], ohT[:, :], zcol[0:32, 0:1].to_broadcast([32, T]),
                0.0, ALU.add, ALU.add), deps=[tprev, tk_zc])
            tcnt = ve.op(lambda e: e.tensor_copy(fs["cnt32"][:],
                                                 crun[:, T - 1:T]), deps=[tscan])
            ptc = pe.op(lambda e: e.transpose(ps_a[0:1, 0:32], fs["cnt32"][:, 0:1],
                                              identt[0:32, 0:32]), deps=[tcnt],
                        inc=True)
            tcr = ve.op(lambda e: e.tensor_copy(cntrow[:], ps_a[0:1, 0:32]),
                        deps=[ptc])
            tdma_cnt = sq.dma(cnt_src[0:1, :], cntrow[0:1, :], deps=[tcr, sq.tok()])
            cc_ag = k.collective("AllGather", [cnt_src[:, :].opt()],
                                 [cnt_dst[:, :].opt()], deps=[tdma_cnt])
            tb1 = sq.dma(b32a[:], cntrow[0:1, :], deps=[tcr, sq.tok()])
            tb2 = sq.dma(cnt84[:], b32a.rearrange("(d j) -> d j", d=8), deps=[tb1, sq.tok()])
            tcum = ve.op(lambda e: e.tensor_tensor_scan(
                m84a[:], cnt84[:], zcol[0:8, 0:1].to_broadcast([8, 4]),
                0.0, ALU.add, ALU.add), deps=[tb2])
            ve.op(lambda e: e.tensor_sub(m84a[:], m84a[:], cnt84[:]))
            tBm = ve.op(lambda e: e.tensor_add(m84a[:], m84a[:], dstb84m1[:]))
            tb3 = sq.dma(b32b[:], m84a[:, :], deps=[tBm, sq.tok()])
            tb4 = sq.dma(fs["Bm1"][:, :], b32b[:], deps=[tb3, sq.tok()])
            def strip_cols(col_ap, dst, deps0):
                tp = ve.tok()
                for s in range(8):
                    tstt = ve.op(lambda e, s=s: e.scalar_tensor_tensor(
                        comb_s[:, :], crun[:, s * 512:(s + 1) * 512], col_ap,
                        ohT[:, s * 512:(s + 1) * 512], ALU.add, ALU.mult),
                        deps=list(deps0) + [tp])
                    pm = pe.op(lambda e: e.matmul(
                        ps_a[0:1, 0:512], ones32c[:, 0:1], comb_s[:, :],
                        start=True, stop=True), deps=[tstt], inc=True)
                    trow = ve.op(lambda e: e.tensor_copy(rowscr[:, :],
                                                         ps_a[0:1, 0:512]),
                                 deps=[pm])
                    for q in range(4):
                        pe.op(lambda e, q=q: e.transpose(
                            ps_b[0:128, q:q + 1],
                            rowscr[0:1, q * 128:(q + 1) * 128],
                            identt[0:1, 0:1]),
                            deps=([trow] if q == 0 else ()), inc=(q == 3))
                    ptr = pe.tok()
                    tp = ve.op(lambda e, s=s: e.tensor_copy(
                        dst[:, s * 4:(s + 1) * 4], ps_b[0:128, 0:4]), deps=[ptr])
                return tp
            tprev2 = strip_cols(fs["Bm1"][:, 0:1], slot_c, [tb4, tscan])
            if DBG and call == 1:
                sq.dma(dbg_strip[:, 0:512], comb_s[:, :], deps=[tprev2])
                sq.dma(dbg_strip[0:1, 512:1024], rowscr[0:1, :], deps=[tprev2])
                sq.dma(dbg_strip[:, 1024:1025], fs["Bm1"][:, :], deps=[tprev2],
                       allow_slow_non_contiguous=True)
            tld = sq.dma(cntd_sb[:], cnt_dst[:, :], deps=[cc_ag, sq.tok()])
            tmask = ve.op(lambda e: e.tensor_scalar(
                fs["cntd_m"][:], cntd_sb[:], mprev[:, 0:1], None, ALU.mult),
                deps=[tld])
            pbm = pe.op(lambda e: e.matmul(ps_b[0:32, 0:1], fs["cntd_m"][:],
                                           ones8c[:, 0:1], start=True, stop=True),
                        deps=[tmask], inc=True)
            tbase = ve.op(lambda e: e.tensor_scalar(basem1[:], ps_b[0:32, 0:1],
                                                    1.0, None, ALU.subtract),
                          deps=[pbm])
            tprev3 = strip_cols(basem1[:, 0:1], posg_c, [tbase, tprev2])
            tacc = ve.op(lambda e: e.tensor_scalar(
                acc_c[:], posg_c[:], float(CAP), None, ALU.is_lt), deps=[tprev3])
            ve.op(lambda e: e.scalar_tensor_tensor(
                gfin_c[:], gate_c[:], 0.5, acc_c[:], ALU.mult, ALU.mult),
                deps=[tacc])
            ve.op(lambda e: e.tensor_scalar(gw_c[:], slot_c[:], ZROW, None,
                                            ALU.subtract), deps=[tprev2])
            ve.op(lambda e: e.tensor_mul(gw_c[:], gw_c[:], acc_c[:]))
            ve.op(lambda e: e.tensor_scalar(gw_c[:], gw_c[:], ZROW, None, ALU.add))
            ve.op(lambda e: e.tensor_copy(gret_u[:], gw_c[:]))
            ve.op(lambda e: e.tensor_copy(slot_u[:], slot_c[:]))
            tk_cols = ve.tok()

            tok_scat = gp.tok()
            for i in range(NT):
                tld = sq.dma(xb_t[:], xln_dr[i * 128:(i + 1) * 128, :],
                             deps=[tok_xb_dma, tok_scat, ve.tok()])
                tok_scat = gp.iscatter(
                    send_dr[:, :], slot_u[:, i:i + 1], xb_t[:],
                    deps=[tk_cols, tld, tk_zbufs],
                    bounds_check=k.bc_reg, oob_is_err=False)
            cc_send = k.collective("AllToAll", [send_dr[:, :].opt()],
                                   [recv_dr[0:AR, :].opt()], deps=[tok_scat])

            tE = sq.dma(cnteT[:], cnt_dst.rearrange("s e -> e s"),
                        deps=[cc_ag, sq.tok()],
                        allow_slow_non_contiguous=True)
            p1 = pe.op(lambda e: e.matmul(ps_a[0:4, 0:8], osel[:], cnteT[:],
                                          start=True, stop=True),
                       deps=[tE, tk_t0, ve.tok()], inc=True)
            tm48 = ve.op(lambda e: e.tensor_copy(m48a[:], ps_a[0:4, 0:8]), deps=[p1])
            p2 = pe.op(lambda e: e.matmul(ps_b[0:8, 0:4], cnteT[:], osel[:],
                                          start=True, stop=True),
                       deps=[tE, ve.tok()], inc=True)
            tm84 = ve.op(lambda e: e.tensor_copy(m84b[:], ps_b[0:8, 0:4]), deps=[p2])
            tc48 = ve.op(lambda e: e.tensor_tensor_scan(
                m48b[:], m48a[:], zcol[0:4, 0:1].to_broadcast([4, 8]),
                0.0, ALU.add, ALU.add), deps=[tm48])
            tcx48 = ve.op(lambda e: e.tensor_sub(m48b[:], m48b[:], m48a[:]),
                          deps=[tc48])
            tc84 = ve.op(lambda e: e.tensor_tensor_scan(
                cnt84[:], m84b[:], zcol[0:8, 0:1].to_broadcast([8, 4]),
                0.0, ALU.add, ALU.add), deps=[tm84])
            tso84 = ve.op(lambda e: e.tensor_sub(cnt84[:], cnt84[:], m84b[:]),
                          deps=[tc84])
            p3 = pe.op(lambda e: e.transpose(ps_c[0:4, 0:8], cnt84[0:8, 0:4],
                                             identt[0:8, 0:8]), deps=[tso84],
                       inc=True)
            tsoT = ve.op(lambda e: e.tensor_copy(m48c[:], ps_c[0:4, 0:8]), deps=[p3])
            td1 = ve.op(lambda e: e.tensor_add(m48c[:], m48c[:], sbase48[:]),
                        deps=[tsoT])
            td2 = ve.op(lambda e: e.tensor_sub(m48c[:], m48c[:], m48b[:]),
                        deps=[td1, tcx48])
            td3 = ve.op(lambda e: e.tensor_copy(m48a[:, 0:1], m48c[:, 0:1]),
                        deps=[td2])
            td4 = ve.op(lambda e: e.tensor_sub(m48a[:, 1:8], m48c[:, 1:8],
                                               m48c[:, 0:7]), deps=[td3])
            tb5 = sq.dma(b32b[:], m48b[:, :], deps=[td4, sq.tok()])
            tb6 = sq.dma(thr_col[:, :], b32b[:], deps=[tb5, sq.tok()])
            tb7 = sq.dma(b32c[:], m48a[:, :], deps=[td4, sq.tok()])
            tb8 = sq.dma(dlt_col[:, :], b32c[:], deps=[tb7, sq.tok()])
            tprev4 = ve.tok()
            for (o, n) in BLK:
                tm1 = ve.op(lambda e, o=o, n=n: e.tensor_scalar(
                    comb_s[:, 0:n], iotaJt[:, o:o + n], thr_col[:, 0:1], None,
                    ALU.is_ge), deps=[tb6, tb8, tk_t0, tprev4])
                tm2 = ve.op(lambda e, n=n: e.tensor_scalar(
                    comb_s[:, 0:n], comb_s[:, 0:n], dlt_col[:, 0:1], None,
                    ALU.mult))
                pm = pe.op(lambda e, n=n: e.matmul(
                    ps_b[0:4, 0:n], bselt[:], comb_s[:, 0:n],
                    start=True, stop=True), deps=[tm2], inc=True)
                tprev4 = ve.op(lambda e, o=o, n=n: e.tensor_tensor(
                    idx4[0:4, o:o + n], ps_b[0:4, 0:n], iotaJt[0:4, o:o + n],
                    ALU.add), deps=[pm])
            tclamp = ve.op(lambda e: e.tensor_scalar_min(idx4[:, :], idx4[:, :], ZROW),
                           deps=[tprev4])
            tok_idxev = ve.tok()
            for g in range(NG):
                pt = pe.op(lambda e, g=g: e.transpose(
                    ps_a[0:128, 0:4], idx4[0:4, g * 128:(g + 1) * 128],
                    identt[0:4, 0:4]), deps=[tclamp, tok_idxev], inc=True)
                tok_idxev = ve.op(lambda e, g=g: e.tensor_copy(
                    idxcol[:, g * EPC:(g + 1) * EPC], ps_a[:, 0:4]), deps=[pt])
            tk_idxcol = ve.tok()

            if DBG and call == 1:
                dtk = ve.tok()
                sq.dma(dbg_cols[:, 0 * NT:1 * NT], slot_c[:], deps=[tk_cols])
                sq.dma(dbg_cols[:, 1 * NT:2 * NT], posg_c[:], deps=[tk_cols])
                sq.dma(dbg_cols[:, 2 * NT:3 * NT], gate_c[:], deps=[tk_cols])
                sq.dma(dbg_cols[:, 3 * NT:4 * NT], gfin_c[:], deps=[tk_cols])
                sq.dma(dbg_cols[:, 4 * NT:5 * NT], acc_c[:], deps=[tk_cols])
                sq.dma(dbg_idx[:, :], idxcol[:], deps=[tk_idxcol])
                sq.dma(dbg_cnt[:, :], cntd_sb[:], deps=[tk_idxcol])
                sq.dma(dbg_sm[:, 0:1], basem1[:, :], deps=[tk_idxcol],
                       allow_slow_non_contiguous=True)
                sq.dma(dbg_sm[:, 1:2], fs["Bm1"][:, :], deps=[tk_idxcol],
                       allow_slow_non_contiguous=True)
                sq.dma(dbg_sm[:, 2:3], thr_col[:, :], deps=[tk_idxcol],
                       allow_slow_non_contiguous=True)
                sq.dma(dbg_sm[:, 3:4], dlt_col[:, :], deps=[tk_idxcol],
                       allow_slow_non_contiguous=True)
                sq.dma(dbg_sm[:, 4:12], cnteT[:, :], deps=[tk_idxcol],
                       allow_slow_non_contiguous=True)
                sq.dma(dbg_sm[0:4, 12:20], m48a[:, :], deps=[tk_idxcol],
                       allow_slow_non_contiguous=True)
                sq.dma(dbg_sm[0:1, 20:24], fs["cnt32"][0:1, 0:1], deps=[tk_idxcol],
                       allow_slow_non_contiguous=True) if False else None

            tok_wload = {"g": sq.tok(), "u": sq.tok(), "d": sq.tok()}
            tok_xe_used = pe.tok()
            tok_wgu_used = pe.tok()
            tok_wd_used = pe.tok()
            tok_a_used = pe.tok()
            psg = (ps_a, ps_b)
            psu = (ps_c, ps_c)
            psy = (ps_y, ps_w)
            tok_g_evac = [sc.tok(), sc.tok()]
            tok_u_evac = [ve.tok(), ve.tok()]
            tok_u_last = ve.tok()
            tok_y_scat = [gp.tok(), gp.tok()]
            tok_y_evac = [sc.tok(), sc.tok()]

            if call == 2 and CUT <= 2:
                return
            k.barrier()
            rctx.close()
            if call == 2 and CUT <= 3:
                return

            gath_toks = {}
            gbuf_free = gp.tok()

            def emit_gathers(e_i, dep):
                for g in range(NG):
                    gath_toks[(e_i, g)] = gp.igather(
                        gbuf2[:, g * H:(g + 1) * H], recv_dr[:, :],
                        idxcol[:, g * EPC + e_i:g * EPC + e_i + 1],
                        deps=[cc_send, tk_idxcol, dep])

            emit_gathers(0, gp.tok())
            for e_i in range(EPC):
                tprevtr = ve.tok()
                for g in range(NG):
                    for c in range(NCH):
                        pe.op(lambda e, g=g, c=c: e.transpose(
                            ps_xeb[:, c * 128:(c + 1) * 128],
                            gbuf2[:, g * H + c * 128:g * H + (c + 1) * 128],
                            identb[:]),
                            deps=([gath_toks[(e_i, g)], tok_xe_used, tprevtr]
                                  if c == 0 else ()),
                            inc=(c == NCH - 1))
                    ptr = pe.tok()
                    tprevtr = ve.op(lambda e, g=g: e.tensor_copy(
                        xe_sb[:, :].rearrange("p (c j) -> p c j", c=NCH)
                        [:, :, g * 128:(g + 1) * 128],
                        ps_xeb[:, :].rearrange("p (c j) -> p c j", c=NCH)),
                        deps=[ptr])
                tk_xe = tprevtr
                if call == 2 and CUT <= 30 + e_i * 10:
                    return
                if e_i + 1 < EPC:
                    emit_gathers(e_i + 1, tprevtr)
                tok_wload["g"] = sq.dma(
                    wg_sb[:, :].rearrange("p (c f) -> p c f", c=NCH),
                    stash["g"][e_i].rearrange("(c p) f -> p c f", p=128),
                    deps=[stash_tok[("g", e_i)], tok_wgu_used])
                tok_wload["u"] = sq.dma(
                    wu_sb[:, :].rearrange("p (c f) -> p c f", c=NCH),
                    stash["u"][e_i].rearrange("(c p) f -> p c f", p=128),
                    deps=[stash_tok[("u", e_i)], tok_wgu_used])
                tok_wload["d"] = sq.dma(
                    wd_sb[:, :].rearrange("p (c f) -> p c f", c=NCH),
                    stash["d"][e_i].rearrange("(c p) f -> p c f", p=128),
                    deps=[stash_tok[("d", e_i)], tok_wd_used])
                for ft in range(NCH):
                    for bi, (o, n) in enumerate(BLK):
                        pp = (ft * 3 + bi) % 2
                        for c in range(NCH):
                            pe.op(lambda e, ft=ft, o=o, n=n, c=c, pp=pp: e.matmul(
                                psg[pp][:, 0:n],
                                wg_sb[:, c * 1024 + ft * 128:c * 1024 + (ft + 1) * 128],
                                xe_sb[:, c * CAP + o:c * CAP + o + n],
                                start=(c == 0), stop=(c == NCH - 1)),
                                deps=([tk_xe, tok_wload["g"], tok_g_evac[pp]]
                                      if c == 0 else ()),
                                inc=(c == NCH - 1))
                        tg = pe.tok()
                        for c in range(NCH):
                            pe.op(lambda e, ft=ft, o=o, n=n, c=c, pp=pp: e.matmul(
                                psu[pp][:, 0:n],
                                wu_sb[:, c * 1024 + ft * 128:c * 1024 + (ft + 1) * 128],
                                xe_sb[:, c * CAP + o:c * CAP + o + n],
                                start=(c == 0), stop=(c == NCH - 1)),
                                deps=([tok_wload["u"], tok_u_last]
                                      if c == 0 else ()),
                                inc=(c == NCH - 1))
                        tu = pe.tok()
                        tok_g_evac[pp] = sc.op(lambda e, n=n, pp=pp: e.activation(
                            sg_sc[pp][:, 0:n], psg[pp][:, 0:n], AF.Silu),
                            deps=[tg, tok_u_evac[pp]])
                        tok_u_evac[pp] = ve.op(
                            lambda e, ft=ft, o=o, n=n, pp=pp: e.tensor_tensor(
                                a_sb[:, ft * CAP + o:ft * CAP + o + n],
                                sg_sc[pp][:, 0:n], psu[pp][:, 0:n], ALU.mult),
                            deps=[tok_g_evac[pp], tu])
                        tok_u_last = tok_u_evac[pp]
                tok_wgu_used = pe.tok()
                tk_a = ve.tok()
                if call == 2 and CUT <= 32 + e_i * 10:
                    return
                for tt in range(NG):
                    pp = tt % 2
                    for half in range(2):
                        for fc in range(NCH):
                            pe.op(lambda e, tt=tt, half=half, fc=fc, pp=pp: e.matmul(
                                psy[pp][:, half * 512:(half + 1) * 512],
                                a_sb[:, fc * CAP + tt * 128:fc * CAP + (tt + 1) * 128],
                                wd_sb[:, fc * 1024 + half * 512:fc * 1024 + (half + 1) * 512],
                                start=(fc == 0), stop=(fc == NCH - 1)),
                                deps=([tk_a, tok_wload["d"], tok_y_evac[pp]]
                                      if (fc == 0 and half == 0) else ()),
                                inc=(fc == NCH - 1 and half == 1))
                    ty = pe.tok()
                    tok_y_evac[pp] = sc.op(lambda e, pp=pp: e.copy(
                        y_sb[pp][:], psy[pp][:, :]), deps=[ty, tok_y_scat[pp]])
                    tok_y_scat[pp] = gp.iscatter(
                        rsrc_dr[:, :], idxcol[:, tt * EPC + e_i:tt * EPC + e_i + 1],
                        y_sb[pp][:], deps=[tok_y_evac[pp]])
                tok_xe_used = pe.tok()
                tok_wd_used = pe.tok()
                tok_a_used = pe.tok()
                if call == 2 and CUT <= 34 + e_i * 10:
                    return
            if call == 2 and CUT <= 4:
                return
            cc_ret = k.collective("AllToAll", [rsrc_dr[0:AR, :].opt()],
                                  [rbuf_dr[0:AR, :].opt()],
                                  deps=[tok_y_scat[0], tok_y_scat[1]])

            tok_y_used = ve.tok()
            tok_out = sq.tok()
            for i in range(NT):
                tgy = gp.igather(yb_t[:], rbuf_dr[:, :], gret_u[:, i:i + 1],
                                 deps=[cc_ret, tok_y_used])
                tyf = ve.op(lambda e: e.tensor_copy(yf_t[:], yb_t[:]), deps=[tgy])
                tok_y_used = tyf
                tys = ve.op(lambda e, i=i: e.tensor_scalar(
                    yf_t[:], yf_t[:], gfin_c[:, i:i + 1], None, ALU.mult),
                    deps=[tyf])
                txl = sq.dma(xb_t[:], xln_dr[i * 128:(i + 1) * 128, :],
                             deps=[ve.tok(), tok_xb_dma])
                txf = ve.op(lambda e: e.tensor_copy(xbf_t[:], xb_t[:]), deps=[txl])
                trt = out_write(i, txf, tys)
            return

        def make_ffn_scope(ctx, tag=""):
            fs = {}
            fs["zcol"] = sb(tag + "f_zc", [128, 1], ctx=ctx)
            fs["xt_f"] = sb(tag + "f_xt", [128, H], ctx=ctx)
            fs["xb_t"] = sb(tag + "f_xb", [128, H], BF16, ctx=ctx)
            fs["sq_t"] = sb(tag + "f_sq", [128, H], ctx=ctx)
            fs["fmsc"] = sb(tag + "f_fm", [128, H], ctx=ctx)
            fs["logit_t"] = sb(tag + "f_lg", [128, NE], ctx=ctx)
            fs["oh_t"] = sb(tag + "f_oh", [128, NE], ctx=ctx)
            fs["msk_t"] = sb(tag + "f_mk", [128, NE], ctx=ctx)
            fs["nrmax"] = sb(tag + "f_nm", [128, 1], ctx=ctx)
            fs["sume"] = sb(tag + "f_se", [128, 1], ctx=ctx)
            fs["gcol"] = sb(tag + "f_gc", [128, 1], ctx=ctx)
            fs["ecol"] = sb(tag + "f_ec", [128, 1], ctx=ctx)
            fs["mu_c"] = sb(tag + "f_mu", [128, NT], ctx=ctx)
            fs["rho_c"] = sb(tag + "f_rh", [128, NT], ctx=ctx)
            fs["slot_c"] = sb(tag + "f_slc", [128, NT], ctx=ctx)
            fs["posg_c"] = sb(tag + "f_pgc", [128, NT], ctx=ctx)
            fs["gate_c"] = sb(tag + "f_gtc", [128, NT], ctx=ctx)
            fs["acc_c"] = sb(tag + "f_acc", [128, NT], ctx=ctx)
            fs["gw_c"] = sb(tag + "f_gwc", [128, NT], ctx=ctx)
            fs["slot_u"] = sb(tag + "f_su", [128, NT], U32, ctx=ctx)
            fs["gret_u"] = sb(tag + "f_gu", [128, NT], U32, ctx=ctx)
            fs["gfin_c"] = sb(tag + "f_gfc", [128, NT], ctx=ctx)
            fs["idxcol"] = sb(tag + "f_ic", [128, NG * EPC], U32, ctx=ctx)
            fs["cnt32"] = sb(tag + "f_c32", [NE, 1], ctx=ctx)
            fs["cntrow"] = sb(tag + "f_crow", [1, NE], ctx=ctx)
            fs["cnt84"] = sb(tag + "f_c84", [8, 4], ctx=ctx)
            fs["m48a"] = sb(tag + "f_m48a", [4, 8], ctx=ctx)
            fs["m48b"] = sb(tag + "f_m48b", [4, 8], ctx=ctx)
            fs["m48c"] = sb(tag + "f_m48c", [4, 8], ctx=ctx)
            fs["m84a"] = sb(tag + "f_m84a", [8, 4], ctx=ctx)
            fs["m84b"] = sb(tag + "f_m84b", [8, 4], ctx=ctx)
            fs["cnteT"] = sb(tag + "f_cT", [NE, 8], ctx=ctx)
            fs["cntd_sb"] = sb(tag + "f_cd", [8, NE], ctx=ctx)
            fs["cntd_m"] = sb(tag + "f_cdm", [8, NE], ctx=ctx)
            fs["thr_col"] = sb(tag + "f_th", [NE, 1], ctx=ctx)
            fs["dlt_col"] = sb(tag + "f_dl", [NE, 1], ctx=ctx)
            fs["basem1"] = sb(tag + "f_bm", [NE, 1], ctx=ctx)
            fs["Bm1"] = sb(tag + "f_Bm1", [NE, 1], ctx=ctx)
            fs["xe_sb"] = sb(tag + "f_xe", [128, NCH * CAP], BF16, ctx=ctx)
            fs["wg_sb"] = sb(tag + "f_wg", [128, NCH * 1024], BF16, ctx=ctx)
            fs["wu_sb"] = sb(tag + "f_wu", [128, NCH * 1024], BF16, ctx=ctx)
            fs["wd_sb"] = sb(tag + "f_wd", [128, NCH * 1024], BF16, ctx=ctx)
            fs["a_sb"] = sb(tag + "f_a", [128, NCH * CAP], BF16, ctx=ctx)
            fs["sg_sc"] = (sb(tag + "f_sg0", [128, 512], ctx=ctx),
                           sb(tag + "f_sg1", [128, 512], ctx=ctx))
            fs["y_sb"] = (sb(tag + "f_y0", [128, 1024], BF16, ctx=ctx),
                          sb(tag + "f_y1", [128, 1024], BF16, ctx=ctx))
            fs["gbuf2"] = sb(tag + "f_gb", [128, NG * H], BF16, ctx=ctx)
            fs["yb_t"] = sb(tag + "f_ybt", [128, H], BF16, ctx=ctx)
            fs["yf_t"] = sb(tag + "f_yft", [128, H], ctx=ctx)
            fs["xbf_t"] = sb(tag + "f_xbf", [128, H], ctx=ctx)
            fs["r_t"] = sb(tag + "f_rt", [128, H], ctx=ctx)
            return fs

        with ExitStack() as c1:
            fs = make_ffn_scope(c1, "c1")
            tok_r2_dma = sq.tok()

            def make_tile1(i, fs=fs):
                return sq.dma(fs["xt_f"][:], x_in[i * 128:(i + 1) * 128, :],
                              deps=[ve.tok(), sc.tok(), pe.tok()])


            def out_write1(i, txf, tys, fs=fs):
                nonlocal tok_r2_dma
                t1 = ve.op(lambda e: e.scalar_tensor_tensor(
                    fs["r_t"][:], fs["xbf_t"][:], 0.5, fs["xt_f"][:],
                    ALU.mult, ALU.add), deps=[txf, _mk1_load[i], tok_r2_dma])
                t2 = ve.op(lambda e: e.tensor_add(fs["r_t"][:], fs["r_t"][:],
                                                  fs["yf_t"][:]), deps=[tys])
                t3 = ve.op(lambda e: e.tensor_reduce(st1[:], fs["r_t"][:],
                                                     axis=AX.X, op=ALU.add))
                t4 = sc.op(lambda e: e.activation(fs["sq_t"][:], fs["r_t"][:],
                                                  AF.Square, accum_out=st2[:, 0:1]),
                           deps=[t2])
                ve.op(lambda e, i=i: e.tensor_scalar(mu2_c[:, i:i + 1], st1[:],
                                                     1.0 / H, None, ALU.mult),
                      deps=[t3])
                ve.op(lambda e, i=i: e.tensor_tensor(st3[:], mu2_c[:, i:i + 1],
                                                     mu2_c[:, i:i + 1], ALU.mult))
                ve.op(lambda e: e.scalar_tensor_tensor(st3[:], st2[:], 1.0 / H,
                                                       st3[:], ALU.mult,
                                                       ALU.subtract), deps=[t4])
                ve.op(lambda e: e.tensor_scalar(st3[:], st3[:], LN_EPS, None,
                                                ALU.add))
                ve.op(lambda e: e.reciprocal(st4[:], st3[:]))
                sc.op(lambda e, i=i: e.activation(rho2_c[:, i:i + 1], st4[:],
                                                  AF.Sqrt), deps=[ve.tok()])
                tok_r2_dma = sq.dma(r2_dr[i * 128:(i + 1) * 128, :], fs["r_t"][:],
                                    deps=[t2])
                if stage == 1:
                    tok_r2_dma = sq.dma(out_dr[i * 128:(i + 1) * 128, :],
                                        fs["r_t"][:], deps=[t2])
                return tok_r2_dma

            _mk1_load = {}

            def make_tile1_full(i):
                t = make_tile1(i)
                return t

            def out_write1_wrap(i, txf, tys):
                _mk1_load[i] = sq.dma(fs["xt_f"][:],
                                      x_in[i * 128:(i + 1) * 128, :],
                                      deps=[ve.tok(), sc.tok(), pe.tok()])
                return out_write1(i, txf, tys)

            emit_ffn(1, fs, make_tile1_full, out_write1_wrap)
            k.barrier()

        if stage >= 2:
            r2fm_dr = nc.dram_tensor("r2fm_dr", [H, T], F32, kind="Internal").ap()
            with ExitStack() as ca:
                r2fm = sb("a_r2fm", [128, T], ctx=ca)
                fmtile = sb("a_fmt", [128, H], ctx=ca)
                sA = sb("a_sA", [128, T], ctx=ca)
                l0b = sb("a_l0", [128, 2048], ctx=ca)
                h0b = sb("a_h0", [128, 2048], ctx=ca)
                lo2 = sb("a_lo2", [128, 1024], ctx=ca)
                h1b = sb("a_h1", [128, 1024], ctx=ca)
                qb = sb("a_qb", [128, 1024], ctx=ca)
                qT_sb = sb("a_qT", [128, 128], ctx=ca)
                vT_sb = sb("a_vT", [128, 128], ctx=ca)
                kv_sb = sb("a_kv", [128, 64], ctx=ca)
                num_sb = sb("a_num", [128, 1024], ctx=ca)
                den_sb = sb("a_den", [1, 2048], ctx=ca)
                mr2_row = sb("a_mrr", [1, T], ctx=ca)
                rho2_row = sb("a_rhr", [1, T], ctx=ca)
                mr_c = sb("a_mrc", [128, NT], ctx=ca)
                rt_l = sb("a_rt", [128, H], ctx=ca)

                tmr = ve.op(lambda e: e.tensor_mul(mr_c[:], mu2_c[:], rho2_c[:]))
                tok_ev = ve.tok()
                for i in range(NT):
                    pt = pe.op(lambda e, i=i: e.transpose(
                        ps_a[0:1, 0:128], mr_c[:, i:i + 1], identt[:]),
                        deps=[tmr, tok_ev], inc=True)
                    ve.op(lambda e, i=i: e.tensor_copy(
                        mr2_row[0:1, i * 128:(i + 1) * 128], ps_a[0:1, 0:128]),
                        deps=[pt])
                    pt2 = pe.op(lambda e, i=i: e.transpose(
                        ps_a[0:1, 128:256], rho2_c[:, i:i + 1], identt[:]),
                        deps=[ve.tok()], inc=True)
                    tok_ev = ve.op(lambda e, i=i: e.tensor_copy(
                        rho2_row[0:1, i * 128:(i + 1) * 128], ps_a[0:1, 128:256]),
                        deps=[pt2])

                tok_ld = sq.tok()
                tok_ev2 = ve.tok()
                tok_fmdma = sq.tok()
                for i in range(NT):
                    tok_ld = sq.dma(rt_l[:], r2_dr[i * 128:(i + 1) * 128, :],
                                    deps=[tok_ev2])
                    for c in range(NCH):
                        pe.op(lambda e, c=c: e.transpose(
                            ps_y[:, c * 128:(c + 1) * 128],
                            rt_l[:, c * 128:(c + 1) * 128], identt[:]),
                            deps=([tok_ld, tok_ev2] if c == 0 else ()),
                            inc=(c == NCH - 1))
                    ptr = pe.tok()
                    tok_ev2 = ve.op(lambda e: e.tensor_copy(fmtile[:], ps_y[:, :]),
                                    deps=[ptr, tok_fmdma])
                    tok_fmdma = sq.dma(
                        r2fm_dr[:, i * 128:(i + 1) * 128]
                        .rearrange("(c p) t -> p c t", p=128),
                        fmtile[:, :].rearrange("p (c t) -> p c t", c=NCH),
                        deps=[tok_ev2])

                def fcol(fi, c, kk):
                    return fil[:, (fi * NCH + c) * 4 + kk:(fi * NCH + c) * 4 + kk + 1]

                def scol(j, c, kk):
                    return fsyn[:, (j * NCH + c) * 3 + kk:(j * NCH + c) * 3 + kk + 1]

                def conv_a(dst, src, L, fi, c, ch, accum=False):
                    Lo = L // 2
                    t = ch.op(lambda e: (e.scalar_tensor_tensor(
                        dst[:, 0:Lo], src[:, 0:L:2], fcol(fi, c, 1), dst[:, 0:Lo],
                        ALU.mult, ALU.add) if accum else e.tensor_scalar(
                        dst[:, 0:Lo], src[:, 0:L:2], fcol(fi, c, 1), None,
                        ALU.mult)))
                    ch.op(lambda e: e.scalar_tensor_tensor(
                        dst[:, 0:Lo], src[:, 1:L:2], fcol(fi, c, 2), dst[:, 0:Lo],
                        ALU.mult, ALU.add))
                    ch.op(lambda e: e.scalar_tensor_tensor(
                        dst[:, 1:Lo], src[:, 1:L - 2:2], fcol(fi, c, 0),
                        dst[:, 1:Lo], ALU.mult, ALU.add))
                    ch.op(lambda e: e.scalar_tensor_tensor(
                        dst[:, 0:Lo - 1], src[:, 2:L:2], fcol(fi, c, 3),
                        dst[:, 0:Lo - 1], ALU.mult, ALU.add))
                    return ch.tok()

                def conv_s(dst, z, Lh, j, c, ch, first):
                    L = 2 * Lh
                    if first:
                        ch.op(lambda e: e.tensor_scalar(
                            dst[:, 0:L:2], z[:, 0:Lh], scol(j, c, 0), None,
                            ALU.mult))
                    else:
                        ch.op(lambda e: e.scalar_tensor_tensor(
                            dst[:, 0:L:2], z[:, 0:Lh], scol(j, c, 0),
                            dst[:, 0:L:2], ALU.mult, ALU.add))
                    ch.op(lambda e: e.scalar_tensor_tensor(
                        dst[:, 2:L:2], z[:, 0:Lh - 1], fcol(2 + j, c, 0),
                        dst[:, 2:L:2], ALU.mult, ALU.add))
                    ch.op(lambda e: e.scalar_tensor_tensor(
                        dst[:, 0:L - 2:2], z[:, 1:Lh], fcol(2 + j, c, 3),
                        dst[:, 0:L - 2:2], ALU.mult, ALU.add))
                    if first:
                        ch.op(lambda e: e.tensor_scalar(
                            dst[:, 1:L:2], z[:, 0:Lh], scol(j, c, 1), None,
                            ALU.mult))
                    else:
                        ch.op(lambda e: e.scalar_tensor_tensor(
                            dst[:, 1:L:2], z[:, 0:Lh], scol(j, c, 1),
                            dst[:, 1:L:2], ALU.mult, ALU.add))
                    ch.op(lambda e: e.scalar_tensor_tensor(
                        dst[:, 1:L - 2:2], z[:, 1:Lh], scol(j, c, 2),
                        dst[:, 1:L - 2:2], ALU.mult, ALU.add))
                    return ch.tok()

                tok_sdma = sq.tok()
                tok_r2fm_used = ve.tok()
                for c in range(NCH):
                    tld_c = sq.dma(r2fm[:], r2fm_dr[c * 128:(c + 1) * 128, :],
                                   deps=[tok_fmdma, tok_r2fm_used])
                    tprev = ve.tok()
                    for s in range(8):
                        pm1 = pe.op(lambda e, s=s: e.matmul(
                            ps_a[:, 0:512], onesr[0:1, :],
                            mr2_row[0:1, s * 512:(s + 1) * 512], start=True,
                            stop=True), deps=[tok_ev, tok_ev2, tprev], inc=True)
                        pm2 = pe.op(lambda e, s=s: e.matmul(
                            ps_b[:, 0:512], onesr[0:1, :],
                            rho2_row[0:1, s * 512:(s + 1) * 512], start=True,
                            stop=True), deps=[tprev], inc=True)
                        t1 = ve.op(lambda e, s=s: e.tensor_tensor(
                            sA[:, s * 512:(s + 1) * 512],
                            r2fm[:, s * 512:(s + 1) * 512],
                            ps_b[:, 0:512], ALU.mult),
                            deps=[pm2, tok_sdma, tld_c])
                        tprev = ve.op(lambda e, s=s: e.tensor_sub(
                            sA[:, s * 512:(s + 1) * 512],
                            sA[:, s * 512:(s + 1) * 512], ps_a[:, 0:512]),
                            deps=[pm1])
                    tk_x2 = ve.tok()
                    conv_a(l0b, sA, T, 0, c, ve)
                    tk_l0 = ve.tok()
                    conv_a(h0b, sA, T, 1, c, ve)
                    tk_h0 = ve.tok()
                    conv_a(lo2, l0b, 2048, 0, c, ve)
                    tk_lo2 = ve.tok()
                    conv_a(h1b, l0b, 2048, 1, c, ve)
                    tk_h1 = ve.tok()
                    t1 = ve.op(lambda e: e.tensor_scalar(qb[:], lo2[:], 0.0, None,
                                                         ALU.min), deps=[tk_lo2])
                    t2 = sc.op(lambda e: e.activation(qb[:], qb[:], AF.Exp),
                               deps=[t1])
                    t3 = ve.op(lambda e: e.tensor_scalar(num_sb[:], lo2[:], 0.0,
                                                         None, ALU.max))
                    t4 = ve.op(lambda e: e.tensor_add(qb[:], qb[:], num_sb[:]),
                               deps=[t2, t3])
                    t5 = ve.op(lambda e: e.tensor_reduce(st1[:], qb[:], axis=AX.X,
                                                         op=ALU.add), deps=[t4])
                    tok_tr_ev = ve.tok()
                    for tt in range(8):
                        p1 = pe.op(lambda e, tt=tt: e.transpose(
                            ps_a[:, 0:128], qb[:, tt * 128:(tt + 1) * 128],
                            identt[:]), deps=[t4, tok_tr_ev], inc=True)
                        p2 = pe.op(lambda e, tt=tt: e.transpose(
                            ps_a[:, 128:256], lo2[:, tt * 128:(tt + 1) * 128],
                            identt[:]), deps=[], inc=True)
                        e1 = ve.op(lambda e: e.tensor_copy(qT_sb[:], ps_a[:, 0:128]),
                                   deps=[p1])
                        e2 = ve.op(lambda e: e.tensor_copy(vT_sb[:],
                                                           ps_a[:, 128:256]),
                                   deps=[p2])
                        tok_tr_ev = e2
                        pe.op(lambda e, tt=tt: e.matmul(
                            ps_c[0:64, 0:64], qT_sb[:, 0:64], vT_sb[:, 0:64],
                            start=(tt == 0), stop=(tt == 7)),
                            deps=[e1, e2], inc=False)
                        pe.op(lambda e, tt=tt: e.matmul(
                            ps_y[64:128, 0:64], qT_sb[:, 64:128], vT_sb[:, 64:128],
                            start=(tt == 0), stop=(tt == 7)),
                            deps=[], inc=(tt == 7))
                    tkv = pe.tok()
                    ekv = ve.op(lambda e: e.tensor_copy(kv_sb[0:64, :],
                                                        ps_c[0:64, 0:64]),
                                deps=[tkv])
                    ekv = ve.op(lambda e: e.tensor_copy(kv_sb[64:128, :],
                                                        ps_y[64:128, 0:64]),
                                deps=[tkv])
                    pe.op(lambda e: e.matmul(ps_y[0:64, 0:512], kv_sb[0:64, 0:64],
                                             qb[0:64, 0:512], start=True, stop=True),
                          deps=[ekv], inc=False)
                    pe.op(lambda e: e.matmul(ps_y[0:64, 512:1024],
                                             kv_sb[0:64, 0:64], qb[0:64, 512:1024],
                                             start=True, stop=True), inc=False)
                    pe.op(lambda e: e.matmul(ps_y[64:128, 0:512],
                                             kv_sb[64:128, 0:64], qb[64:128, 0:512],
                                             start=True, stop=True), inc=False)
                    pe.op(lambda e: e.matmul(ps_y[64:128, 512:1024],
                                             kv_sb[64:128, 0:64],
                                             qb[64:128, 512:1024],
                                             start=True, stop=True), inc=True)
                    tnum = pe.tok()
                    en = ve.op(lambda e: e.tensor_copy(num_sb[:], ps_y[:, :]),
                               deps=[tnum])
                    pe.op(lambda e: e.matmul(ps_a[0:1, 0:512], st1[0:64, 0:1],
                                             qb[0:64, 0:512], start=True, stop=True),
                          deps=[t5, en], inc=False)
                    pe.op(lambda e: e.matmul(ps_b[0:1, 0:512], st1[0:64, 0:1],
                                             qb[0:64, 512:1024], start=True,
                                             stop=True), inc=False)
                    pe.op(lambda e: e.matmul(ps_y[0:1, 0:512], st1[64:128, 0:1],
                                             qb[64:128, 0:512], start=True,
                                             stop=True), inc=False)
                    pe.op(lambda e: e.matmul(ps_y[0:1, 512:1024], st1[64:128, 0:1],
                                             qb[64:128, 512:1024], start=True,
                                             stop=True), inc=True)
                    tden = pe.tok()
                    d1 = ve.op(lambda e: e.tensor_copy(den_sb[0:1, 0:512],
                                                       ps_a[0:1, 0:512]),
                               deps=[tden])
                    ve.op(lambda e: e.tensor_copy(den_sb[0:1, 512:1024],
                                                  ps_b[0:1, 0:512]))
                    ve.op(lambda e: e.tensor_copy(den_sb[0:1, 1024:1536],
                                                  ps_y[0:1, 0:512]))
                    ve.op(lambda e: e.tensor_copy(den_sb[0:1, 1536:2048],
                                                  ps_y[0:1, 512:1024]))
                    ve.op(lambda e: e.tensor_scalar(den_sb[:], den_sb[:],
                                                    ATTN_EPS, None, ALU.add))
                    trc = ve.op(lambda e: e.reciprocal(den_sb[:], den_sb[:]))
                    pe.op(lambda e: e.matmul(ps_w[0:64, 0:512], onesr[0:1, 0:64],
                                             den_sb[0:1, 0:512], start=True,
                                             stop=True), deps=[trc], inc=False)
                    pe.op(lambda e: e.matmul(ps_w[0:64, 512:1024], onesr[0:1, 0:64],
                                             den_sb[0:1, 512:1024], start=True,
                                             stop=True), inc=False)
                    pe.op(lambda e: e.matmul(ps_w[64:128, 0:512], onesr[0:1, 0:64],
                                             den_sb[0:1, 1024:1536], start=True,
                                             stop=True), inc=False)
                    pe.op(lambda e: e.matmul(ps_w[64:128, 512:1024],
                                             onesr[0:1, 0:64],
                                             den_sb[0:1, 1536:2048], start=True,
                                             stop=True), inc=True)
                    tbc = pe.tok()
                    tat = ve.op(lambda e: e.tensor_tensor(qb[:], num_sb[:],
                                                          ps_w[:, :], ALU.mult),
                                deps=[en, tbc])
                    ve.wait(tk_h1)
                    conv_s(l0b, qb, 1024, 0, c, ve, first=True)
                    conv_s(l0b, h1b, 1024, 1, c, ve, first=False)
                    conv_s(sA, l0b, 2048, 0, c, ve, first=False)
                    ve.wait(tk_h0)
                    conv_s(sA, h0b, 2048, 1, c, ve, first=False)
                    tk_sc = ve.tok()
                    tok_r2fm_used = tk_sc
                    tok_sdma = sq.dma(s_dr[c * 128:(c + 1) * 128, :], sA[:],
                                      deps=[tk_sc])
                k.barrier()

        if stage >= 2:
            with ExitStack() as c2:
                fs2 = make_ffn_scope(c2, "c2")
                gam_b = sb("p_gamb", [128, H], ctx=c2)
                bet_b = sb("p_betb", [128, H], ctx=c2)
                tgr = sq.dma(gam_b[0:1, :], gam_in[:, :])
                tbr = sq.dma(bet_b[0:1, :], bet_in[:, :])
                s_sb = fs2["fmsc"]
                stok_t = sb("p_stok", [128, H], ctx=c2)
                at_t = fs2["r_t"]
                r2t_t = fs2["yf_t"]
                mu4 = sb("p_mu4", [128, 1], ctx=c2)
                rho4 = sb("p_rho4", [128, 1], ctx=c2)

                for bc in (gam_b, bet_b):
                    for s in range(2):
                        pm = pe.op(lambda e, bc=bc, s=s: e.matmul(
                            ps_a[:, 0:512], onesr[0:1, :],
                            bc[0:1, s * 512:(s + 1) * 512], start=True, stop=True),
                            deps=[ve.tok(), tgr, tbr], inc=True)
                        ve.op(lambda e, bc=bc, s=s: e.tensor_copy(
                            bc[:, s * 512:(s + 1) * 512], ps_a[:, 0:512]),
                            deps=[pm])
                tk_gb = ve.tok()

                tok_r3dma = sq.tok()

                def make_tile2(i):
                    nonlocal tok_r3dma
                    tl2 = sq.dma(r2t_t[:], r2_dr[i * 128:(i + 1) * 128, :],
                                 deps=[ve.tok()])
                    tl = sq.dma(s_sb[:, :].rearrange("p (c t) -> p c t", c=NCH),
                                s_dr[:, i * 128:(i + 1) * 128]
                                .rearrange("(c p) t -> p c t", p=128),
                                deps=[ve.tok(), sc.tok(), pe.tok()])
                    for c in range(NCH):
                        pe.op(lambda e, c=c: e.transpose(
                            ps_y[:, c * 128:(c + 1) * 128],
                            s_sb[:, c * 128:(c + 1) * 128], identt[:]),
                            deps=([tl, ve.tok()] if c == 0 else ()),
                            inc=(c == NCH - 1))
                    ptr = pe.tok()
                    tse = ve.op(lambda e: e.tensor_copy(stok_t[:], ps_y[:, :]),
                                deps=[ptr])
                    t1 = ve.op(lambda e: e.tensor_reduce(st1[:], stok_t[:],
                                                         axis=AX.X, op=ALU.add))
                    t2 = sc.op(lambda e: e.activation(fs2["sq_t"][:], stok_t[:],
                                                      AF.Square,
                                                      accum_out=st2[:, 0:1]),
                               deps=[tse])
                    ve.op(lambda e: e.tensor_scalar(mu4[:], st1[:], 1.0 / H, None,
                                                    ALU.mult), deps=[t1])
                    ve.op(lambda e: e.tensor_tensor(st3[:], mu4[:], mu4[:],
                                                    ALU.mult))
                    ve.op(lambda e: e.scalar_tensor_tensor(
                        st3[:], st2[:], 1.0 / H, st3[:], ALU.mult, ALU.subtract),
                        deps=[t2])
                    ve.op(lambda e: e.tensor_scalar(st3[:], st3[:], LN_EPS, None,
                                                    ALU.add))
                    ve.op(lambda e: e.reciprocal(st4[:], st3[:]))
                    tr4 = sc.op(lambda e: e.activation(rho4[:], st4[:], AF.Sqrt),
                                deps=[ve.tok()])
                    ta1 = ve.op(lambda e: e.tensor_scalar(
                        at_t[:], stok_t[:], mu4[:, 0:1], rho4[:, 0:1],
                        ALU.subtract, ALU.mult), deps=[tr4])
                    ta2 = ve.op(lambda e: e.tensor_mul(at_t[:], at_t[:], gam_b[:]),
                                deps=[tk_gb])
                    ta3 = ve.op(lambda e: e.tensor_add(at_t[:], at_t[:], bet_b[:]))
                    tr3 = ve.op(lambda e: e.tensor_add(fs2["xt_f"][:], r2t_t[:],
                                                       at_t[:]),
                                deps=[ta3, tl2, sc.tok(), pe.tok()])
                    tok_r3dma = sq.dma(r3_dr[i * 128:(i + 1) * 128, :],
                                       fs2["xt_f"][:], deps=[tr3])
                    if stage == 2:
                        tok_r3dma = sq.dma(out_dr[i * 128:(i + 1) * 128, :],
                                           fs2["xt_f"][:], deps=[tr3])
                    return tr3

                if stage == 2:
                    for i in range(NT):
                        make_tile2(i)
                    k.barrier()
                else:
                    tok_out2 = sq.tok()

                    def out_write2(i, txf, tys):
                        nonlocal tok_out2
                        tl = sq.dma(fs2["r_t"][:],
                                    r3_dr[i * 128:(i + 1) * 128, :],
                                    deps=[ve.tok(), tok_r3dma])
                        t1 = ve.op(lambda e: e.scalar_tensor_tensor(
                            fs2["r_t"][:], fs2["xbf_t"][:], 0.5, fs2["r_t"][:],
                            ALU.mult, ALU.add), deps=[txf, tl, tok_out2])
                        t2 = ve.op(lambda e: e.tensor_add(
                            fs2["r_t"][:], fs2["r_t"][:], fs2["yf_t"][:]),
                            deps=[tys])
                        tok_out2 = sq.dma(out_dr[i * 128:(i + 1) * 128, :],
                                          fs2["r_t"][:], deps=[t2])
                        return tok_out2

                    emit_ffn(2, fs2, make_tile2, out_write2)
                    k.barrier()

        k.barrier()

        with nc.Block() as block:
            k.emit(block)

    return nc


def kernel(L0_attn_x, L0_attn_h_filter, L0_attn_g_filter, L0_attn_h_synth,
           L0_attn_g_synth, L0_attn_norm_gamma, L0_attn_norm_beta,
           L0_ffn_w_router, L0_ffn_w_gate, L0_ffn_w_up, L0_ffn_w_down):
    os.environ["TRN_ALLOW_ICI_ALL_TO_ALL"] = "1"
    x = np.ascontiguousarray(L0_attn_x, dtype=np.float32)
    nc = build()
    in_maps = []
    for r in range(8):
        mprev = np.zeros((8, 1), np.float32)
        mprev[:r] = 1.0
        osel = np.zeros((NE, EPC), np.float32)
        for j in range(EPC):
            osel[4 * r + j, j] = 1.0
        in_maps.append({
            "x": np.ascontiguousarray(x[r]),
            "w_router": np.ascontiguousarray(L0_ffn_w_router, np.float32),
            "h_filter": np.ascontiguousarray(L0_attn_h_filter, np.float32),
            "g_filter": np.ascontiguousarray(L0_attn_g_filter, np.float32),
            "h_synth": np.ascontiguousarray(L0_attn_h_synth, np.float32),
            "g_synth": np.ascontiguousarray(L0_attn_g_synth, np.float32),
            "gamma": np.ascontiguousarray(L0_attn_norm_gamma, np.float32).reshape(1, H),
            "beta": np.ascontiguousarray(L0_attn_norm_beta, np.float32).reshape(1, H),
            "wg": np.ascontiguousarray(L0_ffn_w_gate[4 * r:4 * r + 4], np.float32),
            "wu": np.ascontiguousarray(L0_ffn_w_up[4 * r:4 * r + 4], np.float32),
            "wd": np.ascontiguousarray(L0_ffn_w_down[4 * r:4 * r + 4], np.float32),
            "mask_prev": mprev,
            "own_sel": osel,
        })
    res = run_bass_kernel_spmd(nc, in_maps, core_ids=list(range(8)))
    out = np.stack([res.results[r]["out"] for r in range(8)], axis=0)
    return out
